# version 02 (final)
# baseline (speedup 1.0000x reference)
import numpy as np
from contextlib import ExitStack
import concourse.bass as bass
import concourse.mybir as mybir
from concourse.alu_op_type import AluOpType
from concourse.bass_utils import run_bass_kernel_spmd

DBG_L3F = False
F32 = mybir.dt.float32
BF16 = mybir.dt.bfloat16
I32 = mybir.dt.int32
AF = mybir.ActivationFunctionType
AX = mybir.AxisListType
OP = AluOpType

D = 1024
T = 2048
NK = D // 128
NTC = T // 512
NCORES = 8
CORE_TO_ROW = [0, 4, 1, 5, 2, 6, 3, 7]

ENG2BLK = {"pe": "tensor", "dve": "vector", "act": "scalar",
           "pool": "gpsimd", "sp": "sync"}


class P:

    def __init__(self, nc, st, strict=True):
        self.nc = nc
        self.st = st
        self.strict = strict
        self.sem = {}
        self.cnt = {}
        for e in ("pe", "dve", "act", "pool", "sp", "dma", "dmb", "dmo", "dmp"):
            self.mksem(e)
        self.ops = {k: [] for k in ENG2BLK}
        self._finalized = False
        self._names = set()

    def mksem(self, name):
        self.sem[name] = self.st.enter_context(self.nc.semaphore(name))
        self.cnt[name] = 0
        return name

    def _uname(self, name):
        i = 0
        n = name
        while n in self._names:
            i += 1
            n = f"{name}_{i}"
        self._names.add(n)
        return n

    def sb(self, name, shape, dtype, stack=None):
        return (stack or self.st).enter_context(
            self.nc.sbuf_tensor(self._uname(name), shape, dtype))

    def ps(self, name, shape, dtype=F32, stack=None):
        return (stack or self.st).enter_context(
            self.nc.psum_tensor(self._uname(name), shape, dtype))

    def op(self, eng, f, waits=(), sem=None, inc=1, serial=None):
        assert not self._finalized
        if sem is None:
            sem = eng
        if serial is None:
            serial = self.strict and eng in ("dve", "act", "pool")
        wd = {}
        for s, v in waits:
            if v > 0:
                wd[s] = max(wd.get(s, 0), v)
        if serial and self.cnt[eng] > 0:
            wd[eng] = max(wd.get(eng, 0), self.cnt[eng])
        self.cnt[sem] += inc
        semh = self.sem[sem] if inc else None

        def emit(e, f=f, wd=wd, inc=inc, semh=semh):
            for s, v in wd.items():
                e.wait_ge(self.sem[s], v)
            ins = f(e)
            if inc:
                ins.then_inc(semh, inc)
        self.ops[eng].append(emit)
        return (sem, self.cnt[sem])

    def wait(self, eng, waits):
        wd = {}
        for s, v in waits:
            if v > 0:
                wd[s] = max(wd.get(s, 0), v)

        def emit(e, wd=wd):
            for s, v in wd.items():
                e.wait_ge(self.sem[s], v)
        self.ops[eng].append(emit)

    def dma(self, eng, out, in_, waits=(), sem="dma", **kw):
        if eng == "pool":
            sem = "dmp"
        return self.op(eng, lambda e: e.dma_start(out=out, in_=in_, **kw),
                       waits=waits, sem=sem, inc=16, serial=False)

    def now(self, sem):
        return (sem, self.cnt[sem])

    def all_now(self):
        return [(s, c) for s, c in self.cnt.items() if c > 0]

    def barrier(self, engines=None):
        snap = self.all_now()
        for eng in (engines or ENG2BLK):
            self.wait(eng, snap)

    def finalize(self):
        assert not self._finalized
        self._finalized = True
        block = self.st.enter_context(self.nc.Block())
        for eng, blkname in ENG2BLK.items():
            ops = self.ops[eng]
            if not ops:
                continue
            dec = getattr(block, blkname)

            def body(e, ops=ops):
                for fn in ops:
                    fn(e)
            dec(body)


class Ctx:

    def __init__(self, p):
        self.p = p
        nc = p.nc
        self.ones_f = p.sb("ones_f", [128, 128], F32)
        self.ones_b = p.sb("ones_b", [128, 512], BF16)
        self.ident = p.sb("ident", [128, 128], BF16)
        self.cst = p.sb("cst", [128, 8], F32)
        t0 = p.op("dve", lambda e: e.memset(self.ones_f[:], 1.0))
        t1 = p.op("dve", lambda e: e.memset(self.ones_b[:], 1.0))
        cvals = [1e-5, 1e-6, float(np.pi / 2), 1e-9, 1.0, 0.0, -1e30, 1e-6]
        ct = []
        for i, v in enumerate(cvals):
            ct.append(p.op("dve", lambda e, i=i, v=v: e.memset(self.cst[:, i:i + 1], v)))
        ia = p.sb("io_a", [128, 128], I32)
        ib = p.sb("io_b", [128, 128], I32)
        t2 = p.op("pool", lambda e: e.iota(ia[:], pattern=[[1, 128]], base=0,
                                           channel_multiplier=0))
        t3 = p.op("pool", lambda e: e.iota(ib[:], pattern=[[0, 128]], base=0,
                                           channel_multiplier=1))
        self.ident_f = p.sb("ident_f", [128, 128], F32)
        t4a = p.op("dve", lambda e: e.tensor_tensor(self.ident_f[:], ia[:], ib[:],
                                                    OP.is_equal), waits=[t2, t3])
        t4 = p.op("dve", lambda e: e.tensor_copy(self.ident[:], self.ident_f[:]),
                  waits=[t4a])
        self.maskA = p.sb("maskA", [128, 128], F32)
        self.maskB = p.sb("maskB", [128, 128], F32)
        m1 = p.op("dve", lambda e: e.tensor_tensor(self.maskA[:], ia[:], ib[:],
                                                   OP.is_gt), waits=[t2, t3])
        m2 = p.op("act", lambda e: e.activation(self.maskA[:], self.maskA[:], AF.Copy,
                                                scale=-1e30), waits=[m1])
        m3 = p.op("dve", lambda e: e.tensor_tensor(self.maskB[:], ia[:], ib[:],
                                                   OP.is_lt), waits=[t2, t3, m2])
        m4 = p.op("act", lambda e: e.activation(self.maskB[:], self.maskB[:], AF.Copy,
                                                scale=-1e30), waits=[m3])
        self.ready = [t0, t1, t4, m2, m4] + ct
        self.PS = [p.ps(f"psb{i}", [128, 512]) for i in range(6)]
        self.rowscr = p.sb("rowscr", [1, 3072], F32)
        self.PSB = p.ps("psbf", [128, 1024], BF16)
        self.pstok = [p.now("pe")] * 6
        self.bstok = p.now("pe")

    def c(self, i):
        return self.cst[:, i:i + 1]


def rep_row(p, C, row_ap, dst_tile, waits, n=T, tag="rep"):
    toks = []
    for tc in range((n + 511) // 512):
        w = min(512, n - tc * 512)
        bank = C.PS[4 + (tc % 2)]
        tmm = p.op("pe", lambda e, tc=tc, w=w, bank=bank: e.matmul(
            bank[:, 0:w], C.ones_f[0:1, :], row_ap[0:1, tc * 512:tc * 512 + w],
            start=True, stop=True), waits=list(waits) + [C.pstok[4 + (tc % 2)]])
        tev = p.op("dve", lambda e, tc=tc, w=w, bank=bank: e.tensor_copy(
            dst_tile[:, tc * 512:tc * 512 + w], bank[:, 0:w]), waits=[tmm])
        C.pstok[4 + (tc % 2)] = tev
        toks.append(tev)
    return toks


def emit_module_ln(p, C, X, xtok, ls, want_ratio=None, tag="ln"):
    sqs = p.sb(tag + "_sq", [128, 512], F32, ls)
    t1 = p.sb(tag + "_t1", [128, 512], F32, ls)
    xn = [p.sb(tag + f"_xn{k}", [128, T], BF16, ls) for k in range(NK)]
    ratio_row = p.sb(tag + "_ratio", [1, T], F32, ls) if want_ratio else None
    RS = lambda i: C.rowscr[0:1, i * 512:(i + 1) * 512]
    xn_toks = []
    ratio_toks = []
    row_free = []
    for tc in range(NTC):
        sl = slice(tc * 512, (tc + 1) * 512)
        b1, b2 = C.PS[0], C.PS[1]
        mm1 = mm2 = None
        for k in range(NK):
            tq = p.op("act", lambda e, k=k, sl=sl: e.activation(
                sqs[:], X[k][:, sl], AF.Square),
                waits=[xtok] + ([mm2] if mm2 else C.ready))
            mm1 = p.op("pe", lambda e, k=k, sl=sl, b1=b1: e.matmul(
                b1[0:1, 0:512], C.ones_f[:, 0:1], X[k][:, sl],
                start=(k == 0), stop=(k == NK - 1)),
                waits=[xtok, C.pstok[0]] + C.ready)
            mm2 = p.op("pe", lambda e, k=k, b2=b2: e.matmul(
                b2[0:1, 0:512], C.ones_f[:, 0:1], sqs[:],
                start=(k == 0), stop=(k == NK - 1)),
                waits=[tq, C.pstok[1]])
        e1 = p.op("act", lambda e: e.activation(RS(0), b1[0:1, 0:512], AF.Copy,
                                                scale=1.0 / D), waits=[mm1] + row_free)
        e2 = p.op("act", lambda e: e.activation(RS(1), b2[0:1, 0:512], AF.Copy,
                                                scale=1.0 / D), waits=[mm2] + row_free)
        C.pstok[0] = e1
        C.pstok[1] = e2
        e3 = p.op("act", lambda e: e.activation(RS(4), RS(0), AF.Square), waits=[e1])
        e4 = p.op("dve", lambda e: e.tensor_tensor(RS(1), RS(1), RS(4), OP.subtract),
                  waits=[e2, e3])
        e5 = p.op("act", lambda e: e.activation(RS(2), RS(1), AF.Ln,
                                                bias=C.cst[0:1, 0:1]), waits=[e4])
        e6 = p.op("act", lambda e: e.activation(RS(2), RS(2), AF.Exp,
                                                scale=-0.5), waits=[e5])
        e7 = p.op("dve", lambda e: e.tensor_tensor(RS(3), RS(0), RS(2), OP.mult),
                  waits=[e6])
        row_free = [e7]
        if want_ratio:
            f1 = p.op("dve", lambda e: e.tensor_tensor(RS(5), RS(1), RS(2), OP.mult),
                      waits=[e6])
            f2 = p.op("dve", lambda e: e.tensor_tensor(RS(5), RS(5), RS(2), OP.mult),
                      waits=[f1])
            f3 = p.op("act", lambda e: e.activation(RS(5), RS(5), AF.Ln,
                                                    bias=want_ratio), waits=[f2])
            f4 = p.op("act", lambda e, sl=sl: e.activation(
                ratio_row[0:1, sl], RS(5), AF.Exp, scale=-0.5), waits=[f3])
            ratio_toks.append(f4)
            row_free.append(f4)
        mr1 = p.op("pe", lambda e: e.matmul(C.PS[3][:, 0:512], C.ones_f[0:1, :], RS(2),
                                            start=True, stop=True),
                   waits=[e6, C.pstok[3]])
        mr2 = p.op("pe", lambda e: e.matmul(C.PS[4][:, 0:512], C.ones_f[0:1, :], RS(3),
                                            start=True, stop=True),
                   waits=[e7, C.pstok[4]])
        row_free = row_free + [mr1, mr2]
        lc1 = lc2 = None
        for k in range(NK):
            lc1 = p.op("dve", lambda e, k=k, sl=sl: e.tensor_tensor(
                t1[:], X[k][:, sl], C.PS[3][:, 0:512], OP.mult), waits=[mr1])
            lc2 = p.op("dve", lambda e, k=k, sl=sl: e.tensor_tensor(
                xn[k][:, sl], t1[:], C.PS[4][:, 0:512], OP.subtract), waits=[mr2])
            xn_toks.append(lc2)
        C.pstok[3] = lc1
        C.pstok[4] = lc2
    return xn, ratio_row, xn_toks + ratio_toks, {}


def emit_l1_ffn(p, C, W, X, xn, xn_toks, ls):
    cw = [p.sb(f"l1f_cw{k}", [128, D], BF16, ls) for k in range(NK)]
    sw = [p.sb(f"l1f_sw{k}", [128, D], BF16, ls) for k in range(NK)]
    wa = p.sb("l1f_wa", [128, D], F32, ls)
    wb = p.sb("l1f_wb", [128, D], F32, ls)
    wtoks = []
    prev = None
    for k in range(NK):
        sl = slice(k * 128, (k + 1) * 128)
        for (src_a, src_b, dst, shift) in ((W["l1f_th"], W["l1f_thn"], cw[k], True),
                                           (W["l1f_ph"], W["l1f_phn"], sw[k], False)):
            la = p.dma("sp", wa[:], src_a[sl, :], waits=[prev] if prev else [])
            lb = p.dma("sp", wb[:], src_b[sl, :], waits=[prev] if prev else [])
            tadd = p.op("dve", lambda e: e.tensor_tensor(wa[:], wa[:], wb[:], OP.add),
                        waits=[la, lb])
            if shift:
                tw = p.op("act", lambda e, dst=dst: e.activation(
                    dst[:], wa[:], AF.Sin, bias=C.cst[:, 2:3]), waits=[tadd] + C.ready)
            else:
                tw = p.op("act", lambda e, dst=dst: e.activation(
                    dst[:], wa[:], AF.Sin), waits=[tadd])
            prev = tw
            wtoks.append(tw)

    scr1 = p.sb("l1f_s1", [128, 512], F32, ls)
    scr2 = p.sb("l1f_s2", [128, 512], F32, ls)
    out_toks = []
    ev = None
    for ot in range(NK):
        for tc in range(NTC):
            sl = slice(tc * 512, (tc + 1) * 512)
            bre, bim = C.PS[0], C.PS[1]
            mm_r = mm_i = None
            for k in range(NK):
                mm_r = p.op("pe", lambda e, k=k, ot=ot, sl=sl: e.matmul(
                    bre[:, 0:512], cw[k][:, ot * 128:(ot + 1) * 128], xn[k][:, sl],
                    start=(k == 0), stop=(k == NK - 1)),
                    waits=wtoks + xn_toks + [C.pstok[0]])
                mm_i = p.op("pe", lambda e, k=k, ot=ot, sl=sl: e.matmul(
                    bim[:, 0:512], sw[k][:, ot * 128:(ot + 1) * 128], xn[k][:, sl],
                    start=(k == 0), stop=(k == NK - 1)),
                    waits=[C.pstok[1]])
            q1 = p.op("act", lambda e: e.activation(scr1[:], bre[:, 0:512], AF.Square),
                      waits=[mm_r] + ([ev] if ev else []))
            C.pstok[0] = q1
            q2 = p.op("act", lambda e: e.activation(scr2[:], bim[:, 0:512], AF.Square),
                      waits=[mm_i])
            C.pstok[1] = q2
            q3 = p.op("dve", lambda e: e.tensor_tensor(scr1[:], scr1[:], scr2[:], OP.add),
                      waits=[q1, q2])
            q4a = p.op("act", lambda e: e.activation(scr1[:], scr1[:], AF.Ln,
                                                     bias=C.cst[:, 3:4]), waits=[q3])
            q4 = p.op("act", lambda e: e.activation(scr2[:], scr1[:], AF.Exp,
                                                    scale=0.5), waits=[q4a])
            ev = p.op("dve", lambda e, ot=ot, sl=sl: e.tensor_tensor(
                X[ot][:, sl], X[ot][:, sl], scr2[:], OP.add), waits=[q4])
            out_toks.append(ev)
    return out_toks



def stage_weights(inputs):
    tr = lambda a: np.ascontiguousarray(np.asarray(a).T)
    asf = lambda a: np.ascontiguousarray(np.asarray(a))
    row = lambda a: np.ascontiguousarray(np.asarray(a).reshape(1, -1))

    def col(a):
        a = np.asarray(a).reshape(-1)
        if a.size < 128:
            return np.ascontiguousarray(a.reshape(-1, 1))
        return np.ascontiguousarray(a.reshape(-1, 128).T)
    S = {}
    S["l0_gln_c"] = col(inputs["L0_attn_g_ln"])
    S["l0_wq"] = asf(inputs["L0_attn_Wq"])
    S["l0_wkv"] = asf(inputs["L0_attn_Wkv"])
    S["l0_wo"] = asf(inputs["L0_attn_Wo"])
    S["l0f_wfc"] = tr(inputs["L0_ffn_W_fc"])
    S["l0f_lnw_c"] = col(inputs["L0_ffn_ln_w"])
    S["l0f_lnb_c"] = col(inputs["L0_ffn_ln_b"])
    S["l0f_wproj"] = tr(inputs["L0_ffn_W_proj"])
    S["l1_inproj"] = asf(inputs["L1_attn_in_proj_w"])
    S["l1_convw"] = asf(np.asarray(inputs["L1_attn_conv_w"]).reshape(2048, 4))
    S["l1_convb_c"] = col(inputs["L1_attn_conv_b"])
    S["l1_xproj"] = asf(inputs["L1_attn_x_proj_w"])
    S["l1_dtw"] = asf(inputs["L1_attn_dt_proj_w"])
    S["l1_dtb_c"] = col(inputs["L1_attn_dt_proj_b"])
    S["l1_alog"] = asf(inputs["L1_attn_A_log"])
    S["l1_dp_c"] = col(inputs["L1_attn_Dp"])
    S["l1_outproj"] = asf(inputs["L1_attn_out_proj_w"])
    S["l1f_th"] = asf(inputs["L1_ffn_theta"])
    S["l1f_thn"] = asf(inputs["L1_ffn_theta_noise"])
    S["l1f_ph"] = asf(inputs["L1_ffn_phi"])
    S["l1f_phn"] = asf(inputs["L1_ffn_phi_noise"])
    for nm, k in (("wq", "Wq"), ("wk", "Wk"), ("wv", "Wv"), ("wo", "Wo"),
                  ("wb", "Wb"), ("wa", "Wa"), ("w1", "W1"), ("w2", "W2")):
        S["l2_" + nm] = asf(inputs["L2_attn_" + k])
    for nm in ("ln1_w", "ln1_b", "ln2_w", "ln2_b", "b2"):
        S["l2_" + nm.replace("_", "") + "_c"] = col(inputs["L2_attn_" + nm])
    S["l2_b1_c"] = col(inputs["L2_attn_b1"])
    S["l2_b1_r"] = row(inputs["L2_attn_b1"])
    S["l2_b2_r"] = row(inputs["L2_attn_b2"])
    S["l2f_wg"] = tr(inputs["L2_ffn_w_gate"])
    S["l2f_wu"] = tr(inputs["L2_ffn_w_up"])
    S["l2f_wd"] = tr(inputs["L2_ffn_w_down"])
    S["l3_beta_c"] = col(inputs["L3_attn_beta"])
    for nm in ("Wq", "Wk", "Wv", "Wo"):
        S["l3_" + nm.lower()] = tr(inputs["L3_attn_" + nm])
    for nm in ("bq", "bk", "bv", "bo"):
        S["l3_" + nm + "_r"] = row(inputs["L3_attn_" + nm])
    S["l3f_n1_c"] = col(inputs["L3_ffn_n1_w"])
    S["l3f_n2_c"] = col(inputs["L3_ffn_n2_w"])
    S["l3f_qkv"] = tr(inputs["L3_ffn_qkv_w"])
    S["l3f_ow"] = tr(inputs["L3_ffn_o_w"])
    S["l3f_gate"] = tr(inputs["L3_ffn_gate_w"])
    S["l3f_up"] = tr(inputs["L3_ffn_up_w"])
    S["l3f_down"] = tr(inputs["L3_ffn_down_w"])
    return S


def declare_weights(nc, S):
    W = {}
    for k, v in S.items():
        dt = I32 if v.dtype == np.int32 else F32
        W[k] = nc.dram_tensor(k, list(v.shape), dt, kind="ExternalInput").ap()
    return W


def load_cols(p, C, dram_ap, n, ls, tag, waits=(), dtype=F32):
    nk = n // 128
    t = p.sb(tag, [128, nk], dtype, ls)
    tk = p.dma("sp", t[:], dram_ap, waits=list(waits))
    return [t[:, k:k + 1] for k in range(nk)], [tk]



def emit_layer(p, C, W, X, xtok, li, ls, branches=("attn", "ffn")):
    want_ratio = None
    if li == 0 and "attn" in branches:
        want_ratio = C.cst[0:1, 0:1]
    if li == 3 and "ffn" in branches:
        want_ratio = C.cst[0:1, 1:2]
    if li == 2 and "attn" in branches:
        want_ratio = C.cst[0:1, 0:1]
    xn, ratio_row, xn_toks, stats = emit_module_ln(p, C, X, xtok, ls,
                                                   want_ratio=want_ratio,
                                                   tag=f"L{li}ln")
    toks = []
    EMITTERS = {
        (0, "ffn"): emit_l0_ffn, (1, "ffn"): emit_l1_ffn, (2, "ffn"): emit_l2_ffn,
        (1, "attn"): emit_l1_attn,
        (2, "attn"): globals().get("emit_l2_attn"),
        (3, "ffn"): globals().get("emit_l3_ffn"),
        (3, "attn"): globals().get("emit_l3_attn"),
    }
    order = ["ffn", "attn"] if li != 3 else ["attn", "ffn"]
    for br in order:
        if br not in branches:
            continue
        with ExitStack() as bs:
            if li == 0 and br == "attn":
                toks += emit_l0_attn(p, C, W, X, xn, ratio_row, xn_toks, bs)
            elif li == 3 and br == "ffn":
                toks += emit_l3_ffn(p, C, W, X, xn, ratio_row, xn_toks, bs)
            elif li == 2 and br == "attn":
                toks += emit_l2_attn(p, C, W, X, xn, ratio_row, xn_toks, bs)
            else:
                toks += EMITTERS[(li, br)](p, C, W, X, xn, xn_toks, bs)
            p.barrier()
    return toks



def build_program(S, layers=(0, 1, 2, 3), branches=("attn", "ffn"), dbg_h=False,
                  dbg_b=False, kv_tail=False):
    nc = bass.Bass("TRN2", target_bir_lowering=False)
    xin = nc.dram_tensor("xin", [D, T], F32, kind="ExternalInput").ap()
    xout = nc.dram_tensor("xout", [D, T], F32, kind="ExternalOutput").ap()
    W = declare_weights(nc, S)
    W["dscr_h"] = nc.dram_tensor("dscr_h", [4096, 2048], BF16,
                                 kind="ExternalOutput" if dbg_h else "Internal").ap()
    W["dscr_x"] = nc.dram_tensor("dscr_x", [1024, 2048], F32, kind="Internal").ap()
    W["dscr_r"] = nc.dram_tensor("dscr_r", [4096], F32, kind="Internal").ap()
    W["dscr_mb"] = nc.dram_tensor("dscr_mb", [2048], BF16, kind="Internal").ap()
    W["dscr_r2"] = nc.dram_tensor("dscr_r2", [2048], F32, kind="Internal").ap()
    W["dscr_b"] = nc.dram_tensor("dscr_b", [10240, 2048], BF16,
                                 kind="ExternalOutput" if dbg_b else "Internal").ap()
    if (3 in layers and "attn" in branches) or kv_tail:
        W["l3kv"] = nc.dram_tensor("l3kv", [2048, 2048], BF16,
                                   kind="ExternalOutput").ap()
    if 3 in layers and "attn" in branches:
        W["l3_pkv"] = nc.dram_tensor("l3_pkv", [2048, 2048], BF16,
                                     kind="ExternalInput").ap()
        W["l3_mask_c"] = nc.dram_tensor("l3_mask_c", [2, 1], F32,
                                        kind="ExternalInput").ap()
    st = ExitStack()
    with st:
        p = P(nc, st)
        C = Ctx(p)
        X = [p.sb(f"x{k}", [128, T], F32) for k in range(NK)]
        for k in range(NK):
            p.dma("sp", X[k][:], xin[k * 128:(k + 1) * 128, :])
        xtok = p.now("dma")
        p.barrier()
        for li in layers:
            with ExitStack() as lsx:
                emit_layer(p, C, W, X, xtok, li, lsx, branches)
                p.barrier()
            xtok = p.now("dma")
        if kv_tail:
            emit_l3_kv_tail(p, C, W, X, xtok)
        for k in range(NK):
            p.dma("sp", xout[k * 128:(k + 1) * 128, :], X[k][:], sem="dmo")
        p.wait("sp", [p.now("dmo")])
        p.finalize()
    return nc


def kernel(**inputs):
    S = stage_weights(inputs)
    x = np.asarray(inputs["L0_attn_x"])
    cores = list(range(NCORES))
    ncA = build_program(S, layers=(0, 1, 2), kv_tail=True)
    in_mapsA = []
    for c in cores:
        m = dict(S)
        m["xin"] = np.ascontiguousarray(x[CORE_TO_ROW[c]].T)
        in_mapsA.append(m)
    resA = run_bass_kernel_spmd(ncA, in_mapsA, core_ids=cores)
    ncB = build_program(S, layers=(3,))
    in_mapsB = []
    for c in cores:
        row = CORE_TO_ROW[c]
        prow = row - 4 if row >= 4 else row + 4
        pc = CORE_TO_ROW.index(prow)
        m = dict(S)
        m["xin"] = np.ascontiguousarray(resA.results[c]["xout"])
        m["l3_pkv"] = np.ascontiguousarray(resA.results[pc]["l3kv"])
        m["l3_mask_c"] = np.full((2, 1), 0.0 if row >= 4 else -1e30, np.float32)
        in_mapsB.append(m)
    resB = run_bass_kernel_spmd(ncB, in_mapsB, core_ids=cores)
    out = np.empty((8, T, D), np.float32)
    for c in cores:
        out[CORE_TO_ROW[c]] = resB.results[c]["xout"].T
    return out


def emit_l2_ffn(p, C, W, X, xn, xn_toks, ls):
    H2 = 3072
    out_toks = []
    scr = p.sb("l2f_scr", [128, 512], F32, ls)
    for g in range(3):
        with ExitStack() as gs:
            wg_t = [p.sb(f"l2f_wg{g}_{k}", [128, 1024], BF16, gs) for k in range(NK)]
            wu_t = [p.sb(f"l2f_wu{g}_{k}", [128, 1024], BF16, gs) for k in range(NK)]
            h_t = [p.sb(f"l2f_h{g}_{k}", [128, T], BF16, gs) for k in range(NK)]
            ldg = []
            for k in range(NK):
                sl = slice(k * 128, (k + 1) * 128)
                hsl = slice(g * 1024, (g + 1) * 1024)
                ldg.append(p.dma("pool", wg_t[k][:], W["l2f_wg"][sl, hsl], sem="dmb"))
                ldg.append(p.dma("pool", wu_t[k][:], W["l2f_wu"][sl, hsl], sem="dmb"))
            ev = None
            for ho in range(NK):
                for tc in range(NTC):
                    sl = slice(tc * 512, (tc + 1) * 512)
                    bg, bu = C.PS[0], C.PS[1]
                    mmg = mmu = None
                    for k in range(NK):
                        mmg = p.op("pe", lambda e, k=k, ho=ho, sl=sl: e.matmul(
                            bg[:, 0:512], wg_t[k][:, ho * 128:(ho + 1) * 128],
                            xn[k][:, sl], start=(k == 0), stop=(k == NK - 1)),
                            waits=ldg + xn_toks + [C.pstok[0]])
                        mmu = p.op("pe", lambda e, k=k, ho=ho, sl=sl: e.matmul(
                            bu[:, 0:512], wu_t[k][:, ho * 128:(ho + 1) * 128],
                            xn[k][:, sl], start=(k == 0), stop=(k == NK - 1)),
                            waits=[C.pstok[1]])
                    q1 = p.op("act", lambda e: e.activation(scr[:], bg[:, 0:512],
                                                            AF.Silu, scale=1.0 / 127.0),
                              waits=[mmg])
                    C.pstok[0] = q1
                    q2 = p.op("dve", lambda e, ho=ho, sl=sl: e.tensor_tensor(
                        h_t[ho][:, sl], scr[:], bu[:, 0:512], OP.mult), waits=[mmu, q1])
                    C.pstok[1] = q2
                    ev = q2
            wd_t = [p.sb(f"l2f_wd{g}_{k}", [128, 1024], BF16, gs) for k in range(NK)]
            ldd = []
            for k in range(NK):
                sl = slice(g * 1024 + k * 128, g * 1024 + (k + 1) * 128)
                ldd.append(p.dma("pool", wd_t[k][:], W["l2f_wd"][sl, :], sem="dmb"))
            for ot in range(NK):
                for tc in range(NTC):
                    sl = slice(tc * 512, (tc + 1) * 512)
                    bd = C.PS[2 + (tc % 2)]
                    mmd = None
                    for k in range(NK):
                        mmd = p.op("pe", lambda e, k=k, ot=ot, sl=sl, bd=bd: e.matmul(
                            bd[:, 0:512], wd_t[k][:, ot * 128:(ot + 1) * 128],
                            h_t[k][:, sl], start=(k == 0), stop=(k == NK - 1)),
                            waits=ldd + [ev, C.pstok[2 + (tc % 2)]])
                    ad = p.op("dve", lambda e, ot=ot, sl=sl, bd=bd: e.scalar_tensor_tensor(
                        X[ot][:, sl], bd[:, 0:512], 1.0 / (127.0 * 127.0), X[ot][:, sl],
                        OP.mult, OP.add), waits=[mmd])
                    C.pstok[2 + (tc % 2)] = ad
                    out_toks.append(ad)
            p.barrier()
    return out_toks


def quant_threshold(p, C, W_ap, nkt, tile_w, ls, tag):
    gcol = p.sb(tag + "_g", [128, nkt], F32, ls)
    thr = p.sb(tag + "_thr", [128, 4], F32, ls)
    N = nkt * 128 * tile_w
    with ExitStack() as bs:
        cw = min(tile_w, 2048)
        b = p.sb(tag + "_b", [128, cw], F32, bs)
        ab = p.sb(tag + "_ab", [128, cw], F32, bs)
        dumb = b
        dcol = p.sb(tag + "_dc", [128, nkt * (tile_w // cw)], F32, bs)
        rtoks = []
        cons = None
        for k in range(nkt):
            ld = p.dma("sp", b[:], W_ap[k * 128:(k + 1) * 128, 0:cw], sem="dmb",
                       waits=[cons] if cons else [])
            rt = p.op("dve", lambda e, k=k: e.tensor_reduce(
                gcol[:, k:k + 1], W_ap is None and None or b[:], axis=AX.X, op=OP.add,
                apply_absolute_value=True), waits=[ld])
            if tile_w > cw:
                for c0 in range(cw, tile_w, cw):
                    ld = p.dma("sp", b[:], W_ap[k * 128:(k + 1) * 128, c0:c0 + cw],
                               sem="dmb", waits=[rt])
                    rt = p.op("dve", lambda e, k=k: e.tensor_reduce(
                        thr[:, 3:4], b[:], axis=AX.X, op=OP.add,
                        apply_absolute_value=True), waits=[ld])
                    rt = p.op("dve", lambda e, k=k: e.tensor_tensor(
                        gcol[:, k:k + 1], gcol[:, k:k + 1], thr[:, 3:4], OP.add),
                        waits=[rt])
            cons = rt
            rtoks.append(rt)
        r1 = p.op("dve", lambda e: e.tensor_reduce(thr[:, 2:3], gcol[:], axis=AX.X,
                                                   op=OP.add), waits=rtoks)
        mm = p.op("pe", lambda e: e.matmul(C.PS[5][:, 0:1], C.ones_f[:], thr[:, 2:3],
                                           start=True, stop=True),
                  waits=[r1, C.pstok[5]] + C.ready)
        g0 = p.op("act", lambda e: e.activation(thr[:, 2:3], C.PS[5][0:128, 0:1],
                                                AF.Copy, scale=1.0 / N), waits=[mm])
        C.pstok[5] = g0
        dtoks = []
        cons = g0
        ic = 0
        for k in range(nkt):
            for c0 in range(0, tile_w, cw):
                ld = p.dma("sp", b[:], W_ap[k * 128:(k + 1) * 128, c0:c0 + cw],
                           sem="dmb", waits=[cons])
                aa = p.op("act", lambda e: e.activation(ab[:], b[:], AF.Abs),
                          waits=[ld, g0])
                dd = p.op("dve", lambda e, ic=ic: e.tensor_scalar(
                    dumb[:], ab[:], thr[:, 2:3], 0.0, OP.subtract, op1=OP.add,
                    accum_out=dcol[:, ic:ic + 1]), waits=[aa])
                cons = dd
                dtoks.append(dd)
                ic += 1
        r2 = p.op("dve", lambda e: e.tensor_reduce(thr[:, 3:4], dcol[:], axis=AX.X,
                                                   op=OP.add), waits=dtoks)
        mm2 = p.op("pe", lambda e: e.matmul(C.PS[5][:, 0:1], C.ones_f[:], thr[:, 3:4],
                                            start=True, stop=True),
                   waits=[r2, C.pstok[5]])
        t0 = p.op("act", lambda e: e.activation(thr[:, 0:1], C.PS[5][0:128, 0:1],
                                                AF.Copy, scale=0.5 / N), waits=[mm2])
        C.pstok[5] = t0
        t1a = p.op("dve", lambda e: e.scalar_tensor_tensor(
            thr[:, 0:1], thr[:, 2:3], 0.5, thr[:, 0:1], OP.mult, OP.add),
            waits=[t0, g0])
        t1 = p.op("dve", lambda e: e.tensor_scalar(thr[:, 0:1], thr[:, 0:1], 5e-7,
                                                   None, OP.add), waits=[t1a])
    t2 = p.op("act", lambda e: e.activation(thr[:, 1:2], thr[:, 0:1], AF.Copy,
                                            scale=-1.0), waits=[t1])
    return thr, [t1, t2]


def quant_tiles(p, C, W_ap, nkt, tile_w, thr, thr_tok, dst, ls, tag,
                wfold_cols=None, bias_cols=None, bias_psum=None, n_ot=None):
    cw = min(tile_w, 2048)
    buf = p.sb(tag + "_ld", [128, cw], F32, ls)
    q1 = p.sb(tag + "_q1", [128, cw], BF16, ls)
    q2 = p.sb(tag + "_q2", [128, cw], BF16, ls)
    toks = []
    prev = None
    bias_last = []
    for k in range(nkt):
        a3 = None
        for c0 in range(0, tile_w, cw):
            csl = slice(c0, c0 + cw)
            ld = p.dma("sp", buf[:], W_ap[k * 128:(k + 1) * 128, csl], sem="dmb",
                       waits=[prev] if prev else list(thr_tok))
            a1 = p.op("dve", lambda e: e.tensor_scalar(
                q1[:], buf[:], thr[:, 0:1], None, OP.is_gt), waits=[ld] + thr_tok)
            a2 = p.op("dve", lambda e: e.tensor_scalar(
                q2[:], buf[:], thr[:, 1:2], None, OP.is_lt), waits=[ld] + thr_tok)
            a3 = p.op("dve", lambda e, k=k, csl=csl: e.tensor_tensor(
                dst[k][:, csl], q1[:], q2[:], OP.subtract), waits=[a1, a2])
            prev = a3
        toks.append(a3)
    if bias_cols is not None:
        for ot in range(n_ot):
            bmm = None
            for k in range(nkt):
                bmm = p.op("pe", lambda e, k=k, ot=ot: e.matmul(
                    bias_psum[:, ot:ot + 1], dst[k][:, ot * 128:(ot + 1) * 128],
                    bias_cols[k], start=(k == 0), stop=(k == nkt - 1)),
                    waits=[toks[k]])
            bias_last.append(bmm)
    if wfold_cols is not None:
        for k in range(nkt):
            a4 = p.op("dve", lambda e, k=k: e.tensor_scalar(
                dst[k][:], dst[k][:], wfold_cols[k], None, OP.mult),
                waits=[toks[k]] + bias_last)
            toks[k] = a4
    if False:
        toks.append(prev)
    return toks, bias_last


def emit_l0_ffn(p, C, W, X, xn, xn_toks, ls):
    DSCR = W["dscr_h"]
    out_toks = []
    NH = 32
    wfc_scope = ExitStack()
    wfcq = [p.sb(f"l0f_wfcq{k}", [128, 4096], BF16, wfc_scope) for k in range(NK)]
    with ExitStack() as qs:
        thr, ttok = quant_threshold(p, C, W["l0f_wfc"], NK, 4096, qs, "l0f_t1")
        qt, _ = quant_tiles(p, C, W["l0f_wfc"], NK, 4096, thr, ttok, wfcq, qs, "l0f_q1")
        p.barrier()
    QW = 256
    with ExitStack() as hs:
        h_t = [p.sb(f"l0f_h{j}", [128, QW], BF16, hs) for j in range(NH)]
        sqs = p.sb("l0f_sqs", [128, QW], BF16, hs)
        t1 = p.sb("l0f_t1s", [128, QW], F32, hs)
        RS = lambda i: C.rowscr[0:1, i * 512:i * 512 + QW]
        for tq in range(T // QW):
            sl = slice(tq * QW, (tq + 1) * QW)
            bh, b1, b2 = C.PS[0], C.PS[1], C.PS[2]
            mm1 = mm2 = None
            gel = None
            for j in range(NH):
                mmh = None
                for k in range(NK):
                    mmh = p.op("pe", lambda e, k=k, j=j, sl=sl: e.matmul(
                        bh[:, 0:QW], wfcq[k][:, j * 128:(j + 1) * 128],
                        xn[k][:, sl], start=(k == 0), stop=(k == NK - 1)),
                        waits=xn_toks + [C.pstok[0]])
                gel = p.op("act", lambda e, j=j: e.activation(
                    h_t[j][:], bh[:, 0:QW], AF.Gelu), waits=[mmh])
                C.pstok[0] = gel
                sq = p.op("act", lambda e, j=j: e.activation(
                    sqs[:], h_t[j][:], AF.Square), waits=[gel] + ([mm2] if mm2 else []))
                mm1 = p.op("pe", lambda e, j=j: e.matmul(
                    b1[0:1, 0:QW], C.ones_b[:, 0:1], h_t[j][:],
                    start=(j == 0), stop=(j == NH - 1)), waits=[gel, C.pstok[1]])
                mm2 = p.op("pe", lambda e, j=j: e.matmul(
                    b2[0:1, 0:QW], C.ones_b[:, 0:1], sqs[:],
                    start=(j == 0), stop=(j == NH - 1)), waits=[sq, C.pstok[2]])
            e1 = p.op("act", lambda e: e.activation(RS(0), b1[0:1, 0:QW], AF.Copy,
                                                    scale=1.0 / 4096), waits=[mm1])
            e2 = p.op("act", lambda e: e.activation(RS(1), b2[0:1, 0:QW], AF.Copy,
                                                    scale=1.0 / 4096), waits=[mm2])
            C.pstok[1] = e1
            C.pstok[2] = e2
            e3 = p.op("act", lambda e: e.activation(RS(4), RS(0), AF.Square), waits=[e1])
            e4 = p.op("dve", lambda e: e.tensor_tensor(RS(1), RS(1), RS(4), OP.subtract),
                      waits=[e2, e3])
            e5 = p.op("act", lambda e: e.activation(RS(2), RS(1), AF.Ln,
                                                    bias=C.cst[0:1, 0:1]), waits=[e4])
            e6 = p.op("act", lambda e: e.activation(RS(2), RS(2), AF.Exp,
                                                    scale=-0.5), waits=[e5])
            e7 = p.op("dve", lambda e: e.tensor_tensor(RS(3), RS(0), RS(2), OP.mult),
                      waits=[e6])
            mr1 = p.op("pe", lambda e: e.matmul(C.PS[3][:, 0:QW], C.ones_f[0:1, :],
                                                RS(2), start=True, stop=True),
                       waits=[e6, C.pstok[3]])
            mr2 = p.op("pe", lambda e: e.matmul(C.PS[4][:, 0:QW], C.ones_f[0:1, :],
                                                RS(3), start=True, stop=True),
                       waits=[e7, C.pstok[4]])
            lc1 = lc2 = None
            for j in range(NH):
                lc1 = p.op("dve", lambda e, j=j: e.tensor_tensor(
                    t1[:], h_t[j][:], C.PS[3][:, 0:QW], OP.mult), waits=[mr1])
                lc2 = p.op("dve", lambda e, j=j: e.tensor_tensor(
                    h_t[j][:], t1[:], C.PS[4][:, 0:QW], OP.subtract), waits=[mr2, lc1])
                p.dma("sp", DSCR[j * 128:(j + 1) * 128, sl], h_t[j][:], waits=[lc2],
                      sem="dmb")
            C.pstok[3] = lc1
            C.pstok[4] = lc2
        hdone = p.now("dmb")
        p.barrier()
    wfc_scope.close()
    with ExitStack() as ps_:
        lnw, lnw_tok = load_cols(p, C, W["l0f_lnw_c"], 4096, ps_, "l0f_w")
        lnb32, lnb_tok = load_cols(p, C, W["l0f_lnb_c"], 4096, ps_, "l0f_b")
        lnbt = p.sb("l0f_bbf", [128, 32], BF16, ps_)
        cb = p.op("dve", lambda e: e.tensor_copy(lnbt[:], lnb32[0].tensor.ap()),
                  waits=lnb_tok)
        lnb = [lnbt[:, k:k + 1] for k in range(32)]
        lnb_tok = [cb]
        pbias = p.ps("l0f_pb", [128, 8])
        wpq = [p.sb(f"l0f_wpq{j}", [128, 1024], BF16, ps_) for j in range(NH)]
        bcol = p.sb("l0f_bc", [128, 8], F32, ps_)
        with ExitStack() as qsx:
            thr2, ttok2 = quant_threshold(p, C, W["l0f_wproj"], NH, 1024, qsx, "l0f_t2")
            p.wait("dve", lnw_tok + lnb_tok)
            p.wait("pe", lnw_tok + lnb_tok)
            qt2, bias_last = quant_tiles(p, C, W["l0f_wproj"], NH, 1024, thr2, ttok2,
                                         wpq, qsx, "l0f_q2",
                                         wfold_cols=lnw, bias_cols=lnb,
                                         bias_psum=pbias, n_ot=NK)
            ebc = p.op("dve", lambda e: e.tensor_copy(bcol[:], pbias[:]),
                       waits=bias_last + qt2)
            p.barrier()
        hh = [p.sb(f"l0f_hh{j}", [128, 256], BF16, ps_) for j in range(NH)]
        for tq in range(T // 256):
            sl = slice(tq * 256, (tq + 1) * 256)
            lds = []
            for j in range(NH):
                lds.append(p.dma("sp", hh[j][:], DSCR[j * 128:(j + 1) * 128, sl],
                                 waits=[hdone] + (out_toks[-8:] if out_toks else []),
                                 sem="dmb"))
            for ot in range(NK):
                bo = C.PS[tq % 2]
                mmp = None
                for j in range(NH):
                    mmp = p.op("pe", lambda e, j=j, ot=ot, bo=bo: e.matmul(
                        bo[:, 0:256], wpq[j][:, ot * 128:(ot + 1) * 128], hh[j][:],
                        start=(j == 0), stop=(j == NH - 1)),
                        waits=qt2 + lds + [C.pstok[tq % 2]])
                ev = p.op("dve", lambda e, ot=ot, sl=sl, bo=bo: e.scalar_tensor_tensor(
                    X[ot][:, sl], bo[:, 0:256], bcol[:, ot:ot + 1], X[ot][:, sl],
                    OP.add, OP.add), waits=[mmp, ebc])
                C.pstok[tq % 2] = ev
                out_toks.append(ev)
        p.barrier()
    return out_toks


def bounce_row(p, C, cols_ap, n_col, dram_1d, dst_row_ap, waits, ls, tag, dtype=F32):
    if dtype == BF16:
        pt = C.PSB
        tr = p.op("pe", lambda e: e.transpose(pt[0:n_col, 0:128], cols_ap, C.ident[:]),
                  waits=list(waits) + [C.bstok])
    else:
        pt = C.PS[5]
        tr = p.op("pe", lambda e: e.transpose(pt[0:n_col, 0:128], cols_ap,
                                              C.ident_f[:]), waits=list(waits) + [C.pstok[5]])
    scr = p.sb(tag + "_tr", [n_col, 128], dtype, ls)
    cp = p.op("dve", lambda e: e.tensor_copy(scr[:], pt[0:n_col, 0:128]), waits=[tr])
    if dtype == BF16:
        C.bstok = cp
    else:
        C.pstok[5] = cp
    wr = p.dma("sp", dram_1d.rearrange("(a b) -> a b", a=n_col), scr[:], waits=[cp],
               sem="dmb")
    rd = p.dma("sp", dst_row_ap, dram_1d.rearrange("(a b) -> a b", a=1), waits=[wr],
               sem="dmb")
    return rd


def emit_mha(p, C, W, xn, ratio_row, xn_toks, ls, wsrcs, wo_ap, fold_cols,
             fold_tok, out_cb, tag):
    out_toks = []
    ls = ExitStack()
    if ratio_row is not None:
        ratio_rep = p.sb(tag + "_ratrep", [128, T], F32, ls)
        rr = rep_row(p, C, ratio_row[0:1, :], ratio_rep, xn_toks, tag=tag + "_rr")
        rat_cols = p.sb(tag + "_ratc", [128, 16], F32, ls)
        drr = W["dscr_r"]
        wb = p.dma("sp", drr[0:2048].rearrange("(a b) -> a b", a=1), ratio_row[0:1, :],
                   waits=xn_toks, sem="dmb")
        rc_toks = []
        for j in range(16):
            rc_toks.append(p.dma("sp", rat_cols[:, j:j + 1],
                                 drr[j * 128:(j + 1) * 128]
                                 .rearrange("(b one) -> b one", one=1),
                                 waits=[wb], sem="dmb"))
    else:
        ratio_rep, rat_cols, rr, rc_toks = None, None, [], []
    CT = [p.sb(tag + f"_ct{i}", [128, T], BF16, ls) for i in range(8)]
    scols = p.sb(tag + "_scols", [128, 16], F32, ls)
    mcols = p.sb(tag + "_mcols", [128, 16], F32, ls)
    mcb = p.sb(tag + "_mcb", [128, 16], BF16, ls)
    rscols = p.sb(tag + "_rsc", [128, 16], F32, ls)
    pmx = p.sb(tag + "_pmx", [128, 4], F32, ls)
    sprt = p.sb(tag + "_sprt", [128, 4], F32, ls)
    biasc = p.sb(tag + "_biasc", [128, 1], F32, ls)
    srep = p.sb(tag + "_srep", [128, T], F32, ls)
    ct_done = []

    for pair in range(8):
        with ExitStack() as hsx:
            wqp = p.sb(tag + "_wqp", [128, 1024], BF16, hsx)
            wkp = p.sb(tag + "_wkp", [128, 1024], BF16, hsx)
            wvp = p.sb(tag + "_wvp", [128, 1024], BF16, hsx)
            ldb = p.sb(tag + "_ldb", [128, 128], F32, hsx)
            vtok = p.sb(tag + "_vtok", [128, T], BF16, hsx)
            PT = p.sb(tag + "_pt", [128, 512], BF16, hsx)
            rs_row = p.sb(tag + "_rsrow", [1, T], F32, hsx)

            wtoks = []
            prev_guard = list(ct_done[-2:])
            prev = None
            srcs = [ws[0:1024, pair * 128:(pair + 1) * 128] for ws in wsrcs]
            for (dst, src) in zip((wqp, wkp, wvp), srcs):
                for k in range(NK):
                    ld = p.dma("sp", ldb[:], src[k * 128:(k + 1) * 128, :], sem="dmb",
                               waits=[prev] if prev else prev_guard)
                    if fold_cols is not None:
                        cv = p.op("act", lambda e, dst=dst, k=k: e.activation(
                            dst[:, k * 128:(k + 1) * 128], ldb[:], AF.Copy,
                            scale=fold_cols[k]), waits=[ld] + fold_tok)
                    else:
                        cv = p.op("act", lambda e, dst=dst, k=k: e.activation(
                            dst[:, k * 128:(k + 1) * 128], ldb[:], AF.Copy),
                            waits=[ld])
                    prev = cv
                    wtoks.append(cv)

            vt = []
            for j in range(16):
                bv = C.PS[4 + (j % 2)]
                mmv = None
                for k in range(NK):
                    mmv = p.op("pe", lambda e, k=k, j=j, bv=bv: e.matmul(
                        bv[:, 0:128], xn[k][:, j * 128:(j + 1) * 128],
                        wvp[:, k * 128:(k + 1) * 128],
                        start=(k == 0), stop=(k == NK - 1)),
                        waits=wtoks + xn_toks + [C.pstok[4 + (j % 2)]])
                if rat_cols is not None:
                    ev = p.op("dve", lambda e, j=j, bv=bv: e.tensor_scalar(
                        vtok[:, j * 128:(j + 1) * 128], bv[:, 0:128],
                        rat_cols[:, j:j + 1], None, OP.mult), waits=[mmv] + rc_toks)
                else:
                    ev = p.op("dve", lambda e, j=j, bv=bv: e.tensor_copy(
                        vtok[:, j * 128:(j + 1) * 128], bv[:, 0:128]), waits=[mmv])
                C.pstok[4 + (j % 2)] = ev
                vt.append(ev)

            for hh in range(2):
                h = pair * 2 + hh
                hstk = ExitStack()
                qa = p.sb(tag + "_qaug", [65, T], BF16, hstk)
                ka = p.sb(tag + "_kaug", [65, T], BF16, hstk)
                prange = slice(hh * 64, hh * 64 + 64)
                pj = []
                for (dstau, wsrc) in ((qa, wqp), (ka, wkp)):
                    for tc in range(NTC):
                        sl = slice(tc * 512, (tc + 1) * 512)
                        bq = C.PS[4 + (tc % 2)]
                        mmq = None
                        for k in range(NK):
                            mmq = p.op("pe",
                                       lambda e, k=k, sl=sl, bq=bq, wsrc=wsrc, hh=hh:
                                       e.matmul(
                                           bq[0:64, 0:512],
                                           wsrc[:, k * 128 + hh * 64:k * 128 + hh * 64 + 64],
                                           xn[k][:, sl], start=(k == 0),
                                           stop=(k == NK - 1)),
                                       waits=wtoks + [C.pstok[4 + (tc % 2)]])
                        if ratio_rep is not None:
                            evq = p.op("dve",
                                       lambda e, sl=sl, bq=bq, dstau=dstau:
                                       e.tensor_tensor(
                                           dstau[0:64, sl], bq[0:64, 0:512],
                                           ratio_rep[0:64, sl], OP.mult),
                                       waits=[mmq] + rr)
                        else:
                            evq = p.op("dve",
                                       lambda e, sl=sl, bq=bq, dstau=dstau:
                                       e.tensor_copy(dstau[0:64, sl],
                                                     bq[0:64, 0:512]),
                                       waits=[mmq])
                        C.pstok[4 + (tc % 2)] = evq
                        pj.append(evq)
                ko = p.op("dve", lambda e, ka=ka: e.memset(ka[64:65, :], 1.0),
                          waits=pj)

                pa = []
                for i in range(16):
                    Li = (i + 1) * 128
                    nch = (Li + 511) // 512
                    mms = []
                    for c in range(nch):
                        w = min(512, Li - c * 512)
                        mm = p.op("pe", lambda e, i=i, c=c, w=w, qa=qa, ka=ka: e.matmul(
                            C.PS[c][:, 0:w], qa[0:64, i * 128:(i + 1) * 128],
                            ka[0:64, c * 512:c * 512 + w],
                            start=True, stop=True),
                            waits=pj + [C.pstok[c]])
                        mms.append(mm)
                    cdi = (i * 128) // 512
                    off = i * 128 - cdi * 512
                    mk_ = p.op("dve", lambda e, cdi=cdi, off=off: e.tensor_tensor(
                        C.PS[cdi][:, off:off + 128], C.PS[cdi][:, off:off + 128],
                        C.maskA[:], OP.add), waits=mms)
                    rds = []
                    for c in range(nch):
                        w = min(512, Li - c * 512)
                        rd = p.op("dve", lambda e, c=c, w=w: e.tensor_reduce(
                            pmx[:, c:c + 1], C.PS[c][:, 0:w], axis=AX.X, op=OP.max),
                            waits=[mk_])
                        rds.append(rd)
                    rd2 = p.op("dve", lambda e, i=i, nch=nch: e.tensor_reduce(
                        mcols[:, i:i + 1], pmx[:, 0:nch], axis=AX.X, op=OP.max,
                        negate=True), waits=rds)
                    cb = p.op("act", lambda e, i=i: e.activation(
                        mcb[:, i:i + 1], mcols[:, i:i + 1], AF.Copy), waits=[rd2])
                    cb2 = p.op("act", lambda e, i=i: e.activation(
                        biasc[:, 0:1], mcb[:, i:i + 1], AF.Copy, scale=0.125),
                        waits=[cb])
                    exs = []
                    for c in range(nch):
                        w = min(512, Li - c * 512)
                        ex = p.op("act", lambda e, c=c, w=w: e.activation(
                            PT[:, 0:w], C.PS[c][:, 0:w], AF.Exp, bias=biasc[:, 0:1],
                            scale=0.125, accum_out=sprt[:, c:c + 1]),
                            waits=[cb2] + (pa[-1:] if pa else []))
                        C.pstok[c] = ex
                        exs.append(ex)
                    sm = p.op("dve", lambda e, i=i, nch=nch: e.tensor_reduce(
                        scols[:, i:i + 1], sprt[:, 0:nch], axis=AX.X, op=OP.add),
                        waits=exs)
                    pa.append(sm)
                rs = p.op("dve", lambda e: e.reciprocal(rscols[:], scols[:]), waits=pa)
                mb = bounce_row(p, C, mcb[:], 16, W["dscr_mb"], qa[64:65, :],
                                pa, hstk, tag + f"_mb{h}", dtype=BF16)
                rb = bounce_row(p, C, rscols[:], 16, W["dscr_r2"], rs_row[0:1, :],
                                [rs], hstk, tag + f"_rs{h}", dtype=F32)
                sreps = []
                for tc in range(NTC):
                    sl = slice(tc * 512, (tc + 1) * 512)
                    mmr = p.op("pe", lambda e, sl=sl, prange=prange: e.matmul(
                        C.PS[5][prange, 0:512], C.ones_f[0:1, 0:64],
                        rs_row[0:1, sl], start=True, stop=True),
                        waits=[rb, C.pstok[5]])
                    cpr = p.op("dve", lambda e, sl=sl, prange=prange: e.tensor_copy(
                        srep[prange, sl], C.PS[5][prange, 0:512]), waits=[mmr])
                    C.pstok[5] = cpr
                    sreps.append(cpr)

                for c in range(NTC):
                    csl = slice(c * 512, (c + 1) * 512)
                    jmax = 4 * c + 3
                    pvlast = None
                    prev_exp = None
                    for j in range(jmax + 1):
                        off = max(0, j * 128 - c * 512)
                        mm = p.op("pe", lambda e, j=j, csl=csl, qa=qa, ka=ka: e.matmul(
                            C.PS[4][:, 0:512], ka[:, j * 128:(j + 1) * 128],
                            qa[:, csl], start=True, stop=True),
                            waits=[mb, ko, C.pstok[4]])
                        if j >= 4 * c:
                            mm = p.op("dve", lambda e, off=off: e.tensor_tensor(
                                C.PS[4][:, off:off + 128], C.PS[4][:, off:off + 128],
                                C.maskB[:], OP.add), waits=[mm])
                        wl = [mm] + ([prev_exp] if prev_exp else [])
                        if off > 0:
                            z = p.op("dve", lambda e, off=off: e.memset(
                                PT[:, 0:off], 0.0), waits=wl)
                            wl.append(z)
                        ex = p.op("act", lambda e, off=off: e.activation(
                            PT[:, off:512], C.PS[4][:, off:512], AF.Exp,
                            scale=0.125), waits=wl)
                        C.pstok[4] = ex
                        pv = p.op("pe", lambda e, j=j, c=c, hh=hh, prange=prange:
                                  e.matmul(
                                      C.PS[c][prange, 0:512],
                                      vtok[:, j * 128 + hh * 64:j * 128 + hh * 64 + 64],
                                      PT[:, 0:512], start=(j == 0), stop=(j == jmax)),
                                  waits=[ex] + vt + ([C.pstok[c]] if j == 0 else []))
                        prev_exp = pv
                        pvlast = pv
                    evc = p.op("dve", lambda e, c=c, csl=csl, prange=prange, pair=pair:
                               e.tensor_tensor(CT[pair][prange, csl],
                                               C.PS[c][prange, 0:512],
                                               srep[prange, csl], OP.mult),
                               waits=[pvlast] + sreps)
                    C.pstok[c] = evc
                    ct_done.append(evc)
                hstk.close()
    with ExitStack() as wos:
        wo_t = [p.sb(tag + f"_wo{k}", [128, 1024], BF16, wos) for k in range(NK)]
        lw = [p.dma("pool", wo_t[k][:], wo_ap[k * 128:(k + 1) * 128, :],
                    sem="dmb", waits=ct_done[-2:]) for k in range(NK)]
        for ot in range(NK):
            for tc in range(NTC):
                sl = slice(tc * 512, (tc + 1) * 512)
                bo = C.PS[tc % 2]
                mmo = None
                for k in range(NK):
                    mmo = p.op("pe", lambda e, k=k, ot=ot, sl=sl, bo=bo: e.matmul(
                        bo[:, 0:512], wo_t[k][:, ot * 128:(ot + 1) * 128],
                        CT[k][:, sl], start=(k == 0), stop=(k == NK - 1)),
                        waits=lw + ct_done + [C.pstok[tc % 2]])
                evo = out_cb(ot, sl, bo[:, 0:512], mmo)
                C.pstok[tc % 2] = evo
                out_toks.append(evo)
        p.barrier()
    ls.close()
    p.barrier()
    return out_toks


def emit_l0_attn(p, C, W, X, xn, ratio_row, xn_toks, ls):
    g_cols, gt = load_cols(p, C, W["l0_gln_c"], D, ls, "l0_g")

    def out_cb(ot, sl, psum_ap, tok):
        return p.op("dve", lambda e: e.tensor_tensor(
            X[ot][:, sl], psum_ap, X[ot][:, sl], OP.add), waits=[tok])
    return emit_mha(p, C, W, xn, ratio_row, xn_toks, ls,
                    (W["l0_wq"], W["l0_wkv"][:, 0:1024], W["l0_wkv"][:, 1024:2048]),
                    W["l0_wo"], g_cols, gt, out_cb, "l0")


def emit_l1_attn(p, C, W, X, xn, xn_toks, ls):
    DSB = W["dscr_b"]
    R_UC, R_YF, R_SZ, R_DT, R_DTU = 0, 2048, 4096, 6144, 8192
    NE = 16
    out_toks = []

    with ExitStack() as pa:
        wet = p.sb("l1_wet", [128, 1024], BF16, pa)
        upad = p.sb("l1_upad", [128, T + 3], F32, pa)
        acc = p.sb("l1_acc", [128, T], F32, pa)
        ucb = p.sb("l1_ucb", [128, T], BF16, pa)
        cwc = p.sb("l1_cwc", [128, 4], F32, pa)
        cbc, cbt = load_cols(p, C, W["l1_convb_c"], 2048, pa, "l1_cb")
        zp = p.op("dve", lambda e: e.memset(upad[:, 0:3], 0.0))
        last_out = None
        lcw = None
        for et in range(32):
            is_u = et < NE
            wts = []
            for k in range(NK):
                wts.append(p.dma("pool", wet[:, k * 128:(k + 1) * 128],
                                 W["l1_inproj"][k * 128:(k + 1) * 128,
                                                et * 128:(et + 1) * 128],
                                 sem="dmb",
                                 waits=[last_out] if (k == 0 and last_out) else []))
            if is_u:
                lcw = p.dma("sp", cwc[:], W["l1_convw"][et * 128:(et + 1) * 128, :],
                            sem="dmb", waits=[last_out] if last_out else [])
            evs = []
            for tc in range(NTC):
                sl = slice(tc * 512, (tc + 1) * 512)
                bq = C.PS[tc % 2]
                mmq = None
                for k in range(NK):
                    mmq = p.op("pe", lambda e, k=k, sl=sl, bq=bq: e.matmul(
                        bq[:, 0:512], wet[:, k * 128:(k + 1) * 128], xn[k][:, sl],
                        start=(k == 0), stop=(k == NK - 1)),
                        waits=wts + xn_toks + [C.pstok[tc % 2]])
                if is_u:
                    ev = p.op("act", lambda e, sl=sl, bq=bq: e.activation(
                        upad[:, 3 + sl.start:3 + sl.stop], bq[:, 0:512], AF.Copy),
                        waits=[mmq, zp])
                else:
                    ev = p.op("act", lambda e, sl=sl, bq=bq: e.activation(
                        ucb[:, sl], bq[:, 0:512], AF.Silu), waits=[mmq])
                C.pstok[tc % 2] = ev
                evs.append(ev)
            if is_u:
                c0 = p.op("dve", lambda e: e.tensor_scalar(
                    acc[:], upad[:, 0:T], cwc[:, 0:1], None, OP.mult),
                    waits=evs + [lcw])
                for j in range(1, 4):
                    c0 = p.op("dve", lambda e, j=j: e.scalar_tensor_tensor(
                        acc[:], upad[:, j:j + T], cwc[:, j:j + 1], acc[:],
                        OP.mult, OP.add), waits=[c0])
                sil = p.op("act", lambda e, et=et: e.activation(
                    ucb[:], acc[:], AF.Silu, bias=cbc[et]), waits=[c0] + cbt)
                row = R_UC + et * 128
            else:
                sil = evs[-1]
                row = R_SZ + (et - NE) * 128
            last_out = p.dma("sp", DSB[row:row + 128, :], ucb[:],
                             waits=[sil] + evs, sem="dmb")
        p.barrier()

    dtloT = p.sb("l1_dtlo", [64, T], BF16, ls)
    BCrow = p.sb("l1_bcrow", [96, T], F32, ls)
    with ExitStack() as pb:
        xpw = p.sb("l1_xpw", [128, 96], BF16, pb)
        ucs = [p.sb(f"l1_ucs{i}", [128, T], BF16, pb) for i in range(2)]
        mms = [None] * NTC
        ucld = [None, None]
        for et in range(NE):
            b = ucs[et % 2]
            lw = p.dma("pool", xpw[:], W["l1_xproj"][et * 128:(et + 1) * 128, :],
                       sem="dmb", waits=mms[-1:] if mms[-1] else [])
            ucld[et % 2] = p.dma("sp", b[:],
                                 DSB[R_UC + et * 128:R_UC + et * 128 + 128, :],
                                 sem="dmb",
                                 waits=[ucld[et % 2]] if False else
                                 ([mms[NTC - 1]] if et >= 2 and mms[NTC - 1] else []))
            for tc in range(NTC):
                sl = slice(tc * 512, (tc + 1) * 512)
                mms[tc] = p.op("pe", lambda e, et=et, sl=sl, tc=tc, b=b: e.matmul(
                    C.PS[tc][0:96, 0:512], xpw[:], b[:, sl],
                    start=(et == 0), stop=(et == NE - 1)),
                    waits=[lw, ucld[et % 2], C.pstok[tc]])
        for tc in range(NTC):
            sl = slice(tc * 512, (tc + 1) * 512)
            e1 = p.op("dve", lambda e, sl=sl, tc=tc: e.tensor_copy(
                dtloT[0:64, sl], C.PS[tc][0:64, 0:512]), waits=[mms[tc]])
            e2 = p.op("dve", lambda e, sl=sl, tc=tc: e.tensor_copy(
                BCrow[64:96, sl], C.PS[tc][64:96, 0:512]), waits=[mms[tc]])
            C.pstok[tc] = e2
        p.barrier()

    with ExitStack() as pc:
        dtw = p.sb("l1_dtws", [64, T], BF16, pc)
        sfa = p.sb("l1_sfa", [128, T], F32, pc)
        sfb = p.sb("l1_sfb", [128, 512], F32, pc)
        dtb_ = p.sb("l1_dtb", [128, T], BF16, pc)
        ucc = p.sb("l1_ucc", [128, T], BF16, pc)
        dbc, dbt = load_cols(p, C, W["l1_dtb_c"], 2048, pc, "l1_db")
        lw = p.dma("pool", dtw[:], W["l1_dtw"], sem="dmb")
        last = None
        w1 = None
        a5 = None
        for et in range(NE):
            lu = p.dma("sp", ucc[:], DSB[R_UC + et * 128:R_UC + et * 128 + 128, :],
                       sem="dmb", waits=[last] if last else [])
            chev = []
            for tc in range(NTC):
                sl = slice(tc * 512, (tc + 1) * 512)
                bq = C.PS[4 + (tc % 2)]
                mm = p.op("pe", lambda e, et=et, sl=sl, bq=bq: e.matmul(
                    bq[:, 0:512], dtw[:, et * 128:(et + 1) * 128], dtloT[0:64, sl],
                    start=True, stop=True),
                    waits=[lw, C.pstok[4 + (tc % 2)]])
                a1 = p.op("act", lambda e, et=et, sl=sl, bq=bq: e.activation(
                    sfa[:, sl], bq[:, 0:512], AF.Relu, bias=dbc[et]),
                    waits=[mm] + dbt + ([a5] if a5 else []) + ([w1] if w1 else []))
                a2 = p.op("act", lambda e, et=et, bq=bq: e.activation(
                    sfb[:], bq[:, 0:512], AF.Abs, bias=dbc[et]), waits=[a1])
                C.pstok[4 + (tc % 2)] = a2
                a3 = p.op("act", lambda e: e.activation(
                    sfb[:], sfb[:], AF.Exp, scale=-1.0), waits=[a2])
                a4 = p.op("act", lambda e: e.activation(
                    sfb[:], sfb[:], AF.Ln, bias=C.cst[:, 4:5]), waits=[a3])
                a5 = p.op("dve", lambda e, sl=sl: e.tensor_tensor(
                    dtb_[:, sl], sfa[:, sl], sfb[:], OP.add), waits=[a4])
                chev.append(a5)
            w1 = p.dma("sp", DSB[R_DT + et * 128:R_DT + et * 128 + 128, :], dtb_[:],
                       waits=chev, sem="dmb")
            du = p.op("dve", lambda e: e.tensor_tensor(ucc[:], dtb_[:], ucc[:],
                                                       OP.mult),
                      waits=chev + [lu])
            last = p.dma("sp", DSB[R_DTU + et * 128:R_DTU + et * 128 + 128, :],
                         ucc[:], waits=[du], sem="dmb")
        p.barrier()

    with ExitStack() as pd:
        A_cols = p.sb("l1_acols", [128, 16 * NE], F32, pd)
        dpc, dpt = load_cols(p, C, W["l1_dp_c"], 2048, pd, "l1_dp")
        alt = []
        for et in range(NE):
            la = p.dma("sp", A_cols[:, et * 16:(et + 1) * 16],
                       W["l1_alog"][et * 128:(et + 1) * 128, :], sem="dmb")
            alt.append(la)
        aneg = p.op("act", lambda e: e.activation(A_cols[:], A_cols[:], AF.Exp),
                    waits=alt)
        aneg = p.op("dve", lambda e: e.tensor_scalar(A_cols[:], A_cols[:], -1.0,
                                                     None, OP.mult), waits=[aneg])
        state = p.sb("l1_state", [128, 16 * NE], F32, pd)
        zst = p.op("dve", lambda e: e.memset(state[:], 0.0))
        dts = [p.sb(f"l1_dts{i}", [128, 512], BF16, pd) for i in range(NE)]
        dtus = [p.sb(f"l1_dtus{i}", [128, 512], BF16, pd) for i in range(NE)]
        ys = [p.sb(f"l1_ys{i}", [128, 512], F32, pd) for i in range(NE)]
        brep = p.sb("l1_brep", [128, 512], BF16, pd)
        crep = p.sb("l1_crep", [128, 512], BF16, pd)
        brs = p.sb("l1_brs", [1, 512], BF16, pd)
        crs = p.sb("l1_crs", [1, 512], BF16, pd)
        dA = p.sb("l1_dA", [128, 512], F32, pd)
        bt = p.sb("l1_bt", [128, 512], F32, pd)
        ht = p.sb("l1_ht", [128, 512], F32, pd)
        ucf = p.sb("l1_ucf", [128, 512], BF16, pd)
        szf = p.sb("l1_szf", [128, 512], BF16, pd)
        yb = p.sb("l1_yb", [128, 512], BF16, pd)
        for tch in range(NTC):
            sl = slice(tch * 512, (tch + 1) * 512)
            dtld = [p.dma("sp", dts[et][:],
                          DSB[R_DT + et * 128:R_DT + et * 128 + 128, sl],
                          sem="dmb") for et in range(NE)]
            dtuld = [p.dma("sp", dtus[et][:],
                           DSB[R_DTU + et * 128:R_DTU + et * 128 + 128, sl],
                           sem="dmb") for et in range(NE)]
            yz = [p.op("dve", lambda e, et=et: e.memset(ys[et][:], 0.0))
                  for et in range(NE)]
            h_last = None
            for n in range(16):
                lb = p.dma("pool", brs[:], BCrow[64 + n:65 + n, sl], sem="dmb")
                lc = p.dma("pool", crs[:], BCrow[80 + n:81 + n, sl], sem="dmb")
                mb_ = p.op("pe", lambda e: e.matmul(C.PS[4][:, 0:512],
                                                    C.ones_b[0:1, 0:128], brs[:],
                                                    start=True, stop=True),
                           waits=[lb, C.pstok[4]])
                mc_ = p.op("pe", lambda e: e.matmul(C.PS[5][:, 0:512],
                                                    C.ones_b[0:1, 0:128], crs[:],
                                                    start=True, stop=True),
                           waits=[lc, C.pstok[5]])
                eb = p.op("dve", lambda e: e.tensor_copy(brep[:], C.PS[4][:, 0:512]),
                          waits=[mb_])
                ec = p.op("dve", lambda e: e.tensor_copy(crep[:], C.PS[5][:, 0:512]),
                          waits=[mc_])
                C.pstok[4] = eb
                C.pstok[5] = ec
                for et in range(NE):
                    sc = 16 * et + n
                    da_ = p.op("act", lambda e, et=et, sc=sc: e.activation(
                        dA[:], dts[et][:], AF.Exp, scale=A_cols[:, sc:sc + 1]),
                        waits=[dtld[et], aneg] + ([h_last] if h_last else []))
                    b_ = p.op("pool", lambda e, et=et: e.tensor_tensor(
                        bt[:], dtus[et][:], brep[:], OP.mult),
                        waits=[dtuld[et], eb] + ([h_last] if h_last else []))
                    h_ = p.op("dve", lambda e, sc=sc: e.tensor_tensor_scan(
                        ht[:], dA[:], bt[:], state[:, sc:sc + 1],
                        OP.mult, OP.add), waits=[da_, b_, zst])
                    st_ = p.op("dve", lambda e, sc=sc: e.tensor_copy(
                        state[:, sc:sc + 1], ht[:, 511:512]), waits=[h_])
                    ym = p.op("pool", lambda e: e.tensor_tensor(
                        bt[:], ht[:], crep[:], OP.mult), waits=[h_, ec])
                    ya = p.op("dve", lambda e, et=et: e.tensor_tensor(
                        ys[et][:], ys[et][:], bt[:], OP.add),
                        waits=[ym, yz[et], st_])
                    h_last = ya
            fprev = None
            for et in range(NE):
                lu = p.dma("sp", ucf[:], DSB[R_UC + et * 128:R_UC + et * 128 + 128,
                                             sl], sem="dmb",
                           waits=[fprev] if fprev else [])
                lz = p.dma("sp", szf[:], DSB[R_SZ + et * 128:R_SZ + et * 128 + 128,
                                             sl], sem="dmb",
                           waits=[fprev] if fprev else [])
                f1 = p.op("dve", lambda e, et=et: e.scalar_tensor_tensor(
                    ys[et][:], ucf[:], dpc[et], ys[et][:], OP.mult, OP.add),
                    waits=[lu, h_last] + dpt)
                f2 = p.op("dve", lambda e, et=et: e.tensor_tensor(
                    yb[:], ys[et][:], szf[:], OP.mult), waits=[f1, lz])
                fprev = p.dma("sp", DSB[R_YF + et * 128:R_YF + et * 128 + 128, sl],
                              yb[:], waits=[f2], sem="dmb")
            p.barrier()

    with ExitStack() as pe_:
        opw = [p.sb(f"l1_opw{i}", [128, 1024], BF16, pe_) for i in range(NE)]
        yfc = [p.sb(f"l1_yfc{i}", [128, 512], BF16, pe_) for i in range(NE)]
        lws = [p.dma("pool", opw[i][:],
                     W["l1_outproj"][i * 128:(i + 1) * 128, :], sem="dmb")
               for i in range(NE)]
        evl = None
        for tc in range(NTC):
            sl = slice(tc * 512, (tc + 1) * 512)
            yld = [p.dma("sp", yfc[i][:],
                         DSB[R_YF + i * 128:R_YF + i * 128 + 128, sl], sem="dmb",
                         waits=[evl] if (evl and i == 0) else [])
                   for i in range(NE)]
            for ot in range(NK):
                bq = C.PS[tc % 2]
                mm = None
                for i in range(NE):
                    mm = p.op("pe", lambda e, i=i, ot=ot, bq=bq: e.matmul(
                        bq[:, 0:512], opw[i][:, ot * 128:(ot + 1) * 128], yfc[i][:],
                        start=(i == 0), stop=(i == NE - 1)),
                        waits=lws + yld + [C.pstok[tc % 2]])
                evl = p.op("dve", lambda e, ot=ot, sl=sl, bq=bq: e.tensor_tensor(
                    X[ot][:, sl], bq[:, 0:512], X[ot][:, sl], OP.add), waits=[mm])
                C.pstok[tc % 2] = evl
                out_toks.append(evl)
        p.barrier()
    return out_toks


def emit_l3_ffn(p, C, W, X, xn, ratio_row, xn_toks, ls):
    out_toks = []
    n1c, n1t = load_cols(p, C, W["l3f_n1_c"], D, ls, "l3f_n1")
    n2c, n2t = load_cols(p, C, W["l3f_n2_c"], D, ls, "l3f_n2")
    X2D = W["dscr_h"]
    st2 = p.sb("l3f_st2", [128, 512], BF16, ls)
    _prev_cb = [None]

    def out_cb(ot, sl, psum_ap, tok):
        t1_ = p.op("dve", lambda e: e.tensor_tensor(
            st2[:], psum_ap, xn[ot][:, sl], OP.add),
            waits=[tok] + xn_toks + ([_prev_cb[0]] if _prev_cb[0] else []))
        t2_ = p.dma("sp", X2D[ot * 128:(ot + 1) * 128, sl], st2[:], waits=[t1_],
                    sem="dmb")
        _prev_cb[0] = t2_
        return t1_

    mt = emit_mha(p, C, W, xn, ratio_row, xn_toks, ls,
                  (W["l3f_qkv"][:, 0:1024], W["l3f_qkv"][:, 1024:2048],
                   W["l3f_qkv"][:, 2048:3072]),
                  W["l3f_ow"], n1c, n1t, out_cb, "l3f")
    mt = mt + [_prev_cb[0]]
    p.barrier()

    ratio2_row = p.sb("l3f_r2row", [1, T], F32, ls)
    sqb = p.sb("l3f_sqb", [128, 512], BF16, ls)
    ldx2 = p.sb("l3f_ldx2", [128, 512], BF16, ls)
    RS = lambda i: C.rowscr[0:1, i * 512:(i + 1) * 512]
    r2toks = []
    for tc in range(NTC):
        sl = slice(tc * 512, (tc + 1) * 512)
        mm2 = None
        sq = None
        for k in range(NK):
            lx = p.dma("sp", ldx2[:], X2D[k * 128:(k + 1) * 128, sl], sem="dmb",
                       waits=[sq] if sq else mt)
            sq = p.op("act", lambda e, k=k, sl=sl: e.activation(
                sqb[:], ldx2[:], AF.Square), waits=[lx] + ([mm2] if mm2 else []))
            mm2 = p.op("pe", lambda e, k=k: e.matmul(
                C.PS[2][0:1, 0:512], C.ones_b[:, 0:1], sqb[:],
                start=(k == 0), stop=(k == NK - 1)), waits=[sq, C.pstok[2]])
        e1 = p.op("act", lambda e, sl=sl: e.activation(RS(0), C.PS[2][0:1, 0:512],
                                                       AF.Ln, scale=1.0 / D,
                                                       bias=C.cst[0:1, 1:2]),
                  waits=[mm2])
        C.pstok[2] = e1
        e2 = p.op("act", lambda e, sl=sl: e.activation(ratio2_row[0:1, sl], RS(0),
                                                       AF.Exp, scale=-0.5),
                  waits=[e1])
        r2toks.append(e2)
    ratio2_rep = p.sb("l3f_r2rep", [128, T], F32, ls)
    rr2 = rep_row(p, C, ratio2_row[0:1, :], ratio2_rep, r2toks, tag="l3f_rr2")
    x2buf = p.sb("l3f_x2b", [128, T], BF16, ls)
    x2n_toks = []
    xkp = None
    for k in range(NK):
        lb2 = p.dma("sp", x2buf[:], X2D[k * 128:(k + 1) * 128, :], sem="dmb",
                    waits=[xkp] if xkp else mt)
        a2 = p.op("dve", lambda e, k=k: e.tensor_tensor(
            X[k][:], X[k][:], x2buf[:], OP.add), waits=[lb2])
        xkp = p.op("dve", lambda e, k=k: e.tensor_tensor(
            xn[k][:], x2buf[:], ratio2_rep[:], OP.mult), waits=[a2] + rr2)
        x2n_toks.append(xkp)
    x2n = xn

    if DBG_L3F:
        for k in range(NK):
            p.dma("sp", W["dscr_b"][k * 128:(k + 1) * 128, :], xn[k][:],
                  waits=x2n_toks, sem="dmb")
    if DBG_L3F:
        for k in range(NK):
            p.dma("sp", W["dscr_b"][k * 128:(k + 1) * 128, :], xn[k][:],
                  waits=x2n_toks, sem="dmb")
    gwf = p.sb("l3f_gwf", [128, 64], BF16, ls)
    glt = p.sb("l3f_glt", [128, 8], F32, ls)
    ge = p.sb("l3f_ge", [128, 8], F32, ls)
    gcl = p.sb("l3f_gcl", [128, 8], F32, ls)
    fcols = p.sb("l3f_fcols", [128, 16], F32, ls)
    gldb = p.sb("l3f_gld", [128, 8], F32, ls)
    gw_toks = []
    prevg = None
    for k in range(NK):
        lg = p.dma("sp", gldb[:], W["l3f_gate"][k * 128:(k + 1) * 128, :], sem="dmb",
                   waits=[prevg] if prevg else [])
        cg = p.op("act", lambda e, k=k: e.activation(
            gwf[:, k * 8:(k + 1) * 8], gldb[:], AF.Copy, scale=n2c[k]),
            waits=[lg] + n2t)
        prevg = cg
        gw_toks.append(cg)
    fct = []
    for j in range(16):
        mmg = None
        for k in range(NK):
            mmg = p.op("pe", lambda e, k=k, j=j: e.matmul(
                C.PS[3][:, 0:8], x2n[k][:, j * 128:(j + 1) * 128],
                gwf[:, k * 8:(k + 1) * 8], start=(k == 0), stop=(k == NK - 1)),
                waits=gw_toks + x2n_toks + [C.pstok[3]])
        g1 = p.op("dve", lambda e: e.tensor_copy(glt[:], C.PS[3][:, 0:8]),
                  waits=[mmg])
        C.pstok[3] = g1
        g2 = p.op("dve", lambda e: e.tensor_reduce(gcl[:, 0:1], glt[:], axis=AX.X,
                                                   op=OP.max, negate=True), waits=[g1])
        g3 = p.op("act", lambda e: e.activation(ge[:], glt[:], AF.Exp,
                                                bias=gcl[:, 0:1],
                                                accum_out=gcl[:, 1:2]), waits=[g2])
        g4 = p.op("dve", lambda e: e.tensor_reduce(gcl[:, 2:3], ge[:], axis=AX.X,
                                                   op=OP.max), waits=[g3])
        g5 = p.op("dve", lambda e: e.tensor_scalar(glt[:], ge[:], gcl[:, 2:3], None,
                                                   OP.is_lt), waits=[g4])
        g6 = p.op("dve", lambda e: e.tensor_tensor(ge[:], ge[:], glt[:], OP.mult),
                  waits=[g5])
        g7 = p.op("dve", lambda e: e.tensor_reduce(gcl[:, 3:4], ge[:], axis=AX.X,
                                                   op=OP.max), waits=[g6])
        g8 = p.op("dve", lambda e: e.tensor_tensor(gcl[:, 4:5], gcl[:, 2:3],
                                                   gcl[:, 3:4], OP.add), waits=[g7])
        g9 = p.op("dve", lambda e: e.reciprocal(gcl[:, 5:6], gcl[:, 1:2]), waits=[g8])
        g10 = p.op("dve", lambda e: e.tensor_tensor(gcl[:, 4:5], gcl[:, 4:5],
                                                    gcl[:, 5:6], OP.mult), waits=[g9])
        g11 = p.op("dve", lambda e: e.tensor_scalar(gcl[:, 5:6], gcl[:, 4:5], 1e-9,
                                                    None, OP.add), waits=[g10])
        g12 = p.op("dve", lambda e: e.reciprocal(gcl[:, 5:6], gcl[:, 5:6]),
                   waits=[g11])
        g13 = p.op("dve", lambda e, j=j: e.tensor_tensor(fcols[:, j:j + 1],
                                                         gcl[:, 4:5], gcl[:, 5:6],
                                                         OP.mult), waits=[g12])
        fct.append(g13)
    factor_row = p.sb("l3f_frow", [1, T], F32, ls)
    fb = bounce_row(p, C, fcols[:], 16, W["dscr_r2"], factor_row[0:1, :], fct, ls,
                    "l3f_fb", dtype=F32)
    factor_rep = p.sb("l3f_frep", [128, T], F32, ls)
    fr = rep_row(p, C, factor_row[0:1, :], factor_rep, [fb], tag="l3f_fr")
    if DBG_L3F:
        p.dma("pool", W["dscr_b"][1024:1025, :], factor_row[0:1, :], waits=[fb])
    p.barrier()

    for hq in range(8):
        with ExitStack() as hs8:
            upw = [p.sb(f"l3f_up{k}", [128, 512], BF16, hs8) for k in range(NK)]
            dnw = [p.sb(f"l3f_dn{i}", [128, 1024], BF16, hs8) for i in range(4)]
            h3 = [p.sb(f"l3f_h3{i}", [128, T], BF16, hs8) for i in range(4)]
            ldb8 = p.sb("l3f_ld8", [128, 512], F32, hs8)
            uw_toks = []
            prevu = None
            for k in range(NK):
                lu = p.dma("sp", ldb8[:],
                           W["l3f_up"][k * 128:(k + 1) * 128,
                                       hq * 512:(hq + 1) * 512], sem="dmb",
                           waits=[prevu] if prevu else (out_toks[-1:] or []))
                cu = p.op("act", lambda e, k=k: e.activation(
                    upw[k][:], ldb8[:], AF.Copy, scale=n2c[k]), waits=[lu] + n2t)
                prevu = cu
                uw_toks.append(cu)
            dw_toks = [p.dma("pool", dnw[i][:],
                             W["l3f_down"][hq * 512 + i * 128:hq * 512 + (i + 1) * 128,
                                           :], sem="dmb") for i in range(4)]
            hts = []
            for ht in range(4):
                for tc in range(NTC):
                    sl = slice(tc * 512, (tc + 1) * 512)
                    bq = C.PS[4 + (tc % 2)]
                    mmu = None
                    for k in range(NK):
                        mmu = p.op("pe", lambda e, k=k, ht=ht, sl=sl, bq=bq: e.matmul(
                            bq[:, 0:512], upw[k][:, ht * 128:(ht + 1) * 128],
                            x2n[k][:, sl], start=(k == 0), stop=(k == NK - 1)),
                            waits=uw_toks + x2n_toks + [C.pstok[4 + (tc % 2)]])
                    sv = p.op("act", lambda e, ht=ht, sl=sl, bq=bq: e.activation(
                        h3[ht][:, sl], bq[:, 0:512], AF.Silu), waits=[mmu])
                    C.pstok[4 + (tc % 2)] = sv
                    fv = p.op("dve", lambda e, ht=ht, sl=sl: e.tensor_tensor(
                        h3[ht][:, sl], h3[ht][:, sl], factor_rep[:, sl], OP.mult),
                        waits=[sv] + fr)
                    hts.append(fv)
            for ot in range(NK):
                for tc in range(NTC):
                    sl = slice(tc * 512, (tc + 1) * 512)
                    bq = C.PS[tc % 2]
                    mmd = None
                    for i in range(4):
                        mmd = p.op("pe", lambda e, i=i, ot=ot, sl=sl, bq=bq: e.matmul(
                            bq[:, 0:512], dnw[i][:, ot * 128:(ot + 1) * 128],
                            h3[i][:, sl], start=(i == 0), stop=(i == 3)),
                            waits=dw_toks + hts + [C.pstok[tc % 2]])
                    ev = p.op("dve", lambda e, ot=ot, sl=sl, bq=bq: e.tensor_tensor(
                        X[ot][:, sl], bq[:, 0:512], X[ot][:, sl], OP.add),
                        waits=[mmd])
                    C.pstok[tc % 2] = ev
                    out_toks.append(ev)
            p.barrier()
    return out_toks


def emit_l3_attn(p, C, W, X, xn, xn_toks, ls):
    out_toks = []
    ctx = [p.sb(f"l3a_ctx{i}", [128, T], BF16, ls) for i in range(NK)]
    EX2 = p.sb("l3a_ex2", [2, 128], BF16, ls)
    blk64 = p.sb("l3a_blk", [128, 2], F32, ls)
    ia2 = p.sb("l3a_ia2", [2, 128], I32, ls)
    ib2 = p.sb("l3a_ib2", [2, 128], I32, ls)
    sh2 = p.sb("l3a_sh2", [2, 1], I32, ls)
    ia3 = p.sb("l3a_ia3", [128, 2], I32, ls)
    ib3 = p.sb("l3a_ib3", [128, 2], I32, ls)
    sh3 = p.sb("l3a_sh3", [128, 1], I32, ls)
    c1 = p.op("pool", lambda e: e.iota(ia2[:], pattern=[[1, 128]], base=0,
                                       channel_multiplier=0))
    c2 = p.op("pool", lambda e: e.iota(ib2[:], pattern=[[0, 128]], base=0,
                                       channel_multiplier=1))
    c3 = p.op("pool", lambda e: e.iota(sh2[:], pattern=[[0, 1]], base=6,
                                       channel_multiplier=0))
    c4 = p.op("dve", lambda e: e.tensor_scalar(ia2[:], ia2[:], sh2[:, 0:1], None,
                                               OP.arith_shift_right), waits=[c1, c3])
    c5 = p.op("dve", lambda e: e.tensor_tensor(EX2[:], ia2[:], ib2[:], OP.is_equal),
              waits=[c4, c2])
    c6 = p.op("pool", lambda e: e.iota(ia3[:], pattern=[[1, 2]], base=0,
                                       channel_multiplier=0))
    c7 = p.op("pool", lambda e: e.iota(ib3[:], pattern=[[0, 2]], base=0,
                                       channel_multiplier=1))
    c8 = p.op("pool", lambda e: e.iota(sh3[:], pattern=[[0, 1]], base=6,
                                       channel_multiplier=0))
    c9 = p.op("dve", lambda e: e.tensor_scalar(ib3[:], ib3[:], sh3[:, 0:1], None,
                                               OP.arith_shift_right), waits=[c7, c8])
    c10 = p.op("dve", lambda e: e.tensor_tensor(blk64[:], ib3[:], ia3[:],
                                                OP.is_equal), waits=[c9, c6])
    cready = [c5, c10]
    bcol = p.sb("l3a_bcol", [16, 1], F32, ls)
    qsc = p.sb("l3a_qsc", [2, 8], F32, ls)
    mcol = p.sb("l3a_mcol", [2, 1], F32, ls)
    lb = p.dma("sp", bcol[:], W["l3_beta_c"])
    lm = p.dma("sp", mcol[:], W["l3_mask_c"])
    b1 = p.op("act", lambda e: e.activation(bcol[:], bcol[:], AF.Exp, scale=-1.0),
              waits=[lb])
    b2 = p.op("dve", lambda e: e.tensor_scalar(bcol[:], bcol[:], 0.125, None,
                                               OP.mult), waits=[b1])
    wbb = p.dma("sp", W["dscr_r2"][0:16].rearrange("(a b) -> a b", a=16), bcol[:],
                waits=[b2], sem="dmb")
    qsct = []
    for ot in range(NK):
        qsct.append(p.dma("sp", qsc[:, ot:ot + 1],
                          W["dscr_r2"][2 * ot:2 * ot + 2]
                          .rearrange("(a b) -> a b", b=1), waits=[wbb], sem="dmb"))
    brows = p.sb("l3a_brows", [1, 3 * 1024], BF16, ls)
    lbr = [p.dma("pool", brows[0:1, i * 1024:(i + 1) * 1024], W[nm])
           for i, nm in enumerate(("l3_bq_r", "l3_bk_r", "l3_bv_r"))]
    bo_row = p.sb("l3a_borow", [1, 1024], BF16, ls)
    lbo = p.dma("pool", bo_row[:], W["l3_bo_r"])

    kvt = []
    for ot in range(NK):
        with ExitStack() as os_:
            qT = p.sb("l3a_q", [128, T], BF16, os_)
            kT = p.sb("l3a_k", [128, T], BF16, os_)
            vT = p.sb("l3a_v", [128, T], BF16, os_)
            wld = p.sb("l3a_wld", [128, 1024], BF16, os_)
            prodc = p.sb("l3a_prod", [128, 512], F32, os_)
            s1 = p.sb("l3a_s1", [2, 1024], F32, os_)
            s2 = p.sb("l3a_s2", [2, 1024], F32, os_)
            w1b = p.sb("l3a_w1", [2, T], BF16, os_)
            w2b = p.sb("l3a_w2", [2, T], BF16, os_)
            pkc = p.sb("l3a_pkc", [128, 512], BF16, os_)
            t1s = p.sb("l3a_t1", [128, 512], F32, os_)
            t2s = p.sb("l3a_t2", [128, 512], F32, os_)
            ptoks = {}
            for wi, (wnm, dst) in enumerate((("l3_wq", qT), ("l3_wk", kT),
                                             ("l3_wv", vT))):
                lw = [p.dma("pool", wld[:, k * 128:(k + 1) * 128],
                            W[wnm][k * 128:(k + 1) * 128, ot * 128:(ot + 1) * 128],
                            waits=[ptoks.get((wi - 1, NTC - 1))] if wi else
                            (out_toks[-1:] + kvt[-2:]))
                      for k in range(NK)]
                for tc in range(NTC):
                    sl = slice(tc * 512, (tc + 1) * 512)
                    bq = C.PS[4 + (tc % 2)]
                    mmq = None
                    for k in range(NK):
                        mmq = p.op("pe", lambda e, k=k, sl=sl, bq=bq: e.matmul(
                            bq[:, 0:512], wld[:, k * 128:(k + 1) * 128],
                            xn[k][:, sl], start=(k == 0), stop=False),
                            waits=lw + xn_toks + [C.pstok[4 + (tc % 2)]] + lbr)
                    mmq = p.op("pe", lambda e, wi=wi, sl=sl, bq=bq, ot=ot: e.matmul(
                        bq[:, 0:512],
                        brows[0:1, wi * 1024 + ot * 128:wi * 1024 + (ot + 1) * 128],
                        C.ones_b[0:1, 0:512], start=False, stop=True),
                        waits=[mmq])
                    ev = p.op("dve", lambda e, sl=sl, bq=bq, dst=dst: e.tensor_copy(
                        dst[:, sl], bq[:, 0:512]), waits=[mmq])
                    C.pstok[4 + (tc % 2)] = ev
                    ptoks[(wi, tc)] = ev
            pall = list(ptoks.values())
            kvo1 = p.dma("sp", W["l3kv"][ot * 128:(ot + 1) * 128, :], kT[:],
                         waits=pall, sem="dmo")
            kvo2 = p.dma("sp", W["l3kv"][1024 + ot * 128:1024 + (ot + 1) * 128, :],
                         vT[:], waits=pall, sem="dmo")
            kvt += [kvo1, kvo2]
            wg1 = wg2 = None
            for th in range(2):
                for (sd, load_partner) in ((s1, False), (s2, True)):
                    for tc2 in range(2):
                        tc = th * 2 + tc2
                        sl = slice(tc * 512, (tc + 1) * 512)
                        if load_partner:
                            lp = p.dma("sp", pkc[:],
                                       W["l3_pkv"][ot * 128:(ot + 1) * 128, sl],
                                       sem="dmb", waits=pall + [C.pstok[2]])
                            kap_ = pkc[:, 0:512]
                            wl = [lp]
                        else:
                            kap_ = kT[:, sl]
                            wl = pall
                        pr = p.op("dve", lambda e, sl=sl, kap_=kap_: e.tensor_tensor(
                            prodc[:], qT[:, sl], kap_, OP.mult),
                            waits=wl + pall + [C.pstok[2]])
                        mmb = p.op("pe", lambda e: e.matmul(
                            C.PS[2][0:2, 0:512], blk64[:], prodc[:],
                            start=True, stop=True),
                            waits=[pr, C.pstok[2]] + cready)
                        evs = p.op("dve", lambda e, tc2=tc2, sd=sd: e.tensor_copy(
                            sd[0:2, tc2 * 512:(tc2 + 1) * 512],
                            C.PS[2][0:2, 0:512]), waits=[mmb])
                        C.pstok[2] = evs
                sc1 = p.op("dve", lambda e, ot=ot: e.tensor_scalar(
                    s1[:], s1[:], qsc[:, ot:ot + 1], None, OP.mult),
                    waits=[C.pstok[2]] + qsct)
                sc2 = p.op("dve", lambda e, ot=ot: e.tensor_scalar(
                    s2[:], s2[:], qsc[:, ot:ot + 1], mcol[:, 0:1], OP.mult,
                    op1=OP.add), waits=[sc1, lm])
                dd = p.op("dve", lambda e: e.tensor_tensor(s1[:], s1[:], s2[:],
                                                           OP.subtract),
                          waits=[sc2])
                wg1 = p.op("act", lambda e, th=th: e.activation(
                    w1b[0:2, th * 1024:(th + 1) * 1024], s1[:], AF.Sigmoid),
                    waits=[dd])
                wg2 = p.op("act", lambda e, th=th: e.activation(
                    w2b[0:2, th * 1024:(th + 1) * 1024], s1[:], AF.Sigmoid,
                    scale=-1.0), waits=[wg1])
            for tc in range(NTC):
                sl = slice(tc * 512, (tc + 1) * 512)
                m1 = p.op("pe", lambda e, sl=sl: e.matmul(
                    C.PS[2][:, 0:512], EX2[:], w1b[0:2, sl], start=True, stop=True),
                    waits=[wg1, C.pstok[2]] + cready)
                m2 = p.op("pe", lambda e, sl=sl: e.matmul(
                    C.PS[3][:, 0:512], EX2[:], w2b[0:2, sl], start=True, stop=True),
                    waits=[wg2, C.pstok[3]])
                lpv = p.dma("sp", pkc[:], W["l3_pkv"][1024 + ot * 128:
                                                      1024 + (ot + 1) * 128, sl],
                            sem="dmb", waits=[C.pstok[3], wg2])
                u1 = p.op("dve", lambda e, sl=sl: e.tensor_tensor(
                    t1s[:], C.PS[2][:, 0:512], vT[:, sl], OP.mult), waits=[m1])
                C.pstok[2] = u1
                u2 = p.op("dve", lambda e, sl=sl: e.tensor_tensor(
                    t2s[:], C.PS[3][:, 0:512], pkc[:, 0:512], OP.mult),
                    waits=[m2, lpv])
                C.pstok[3] = u2
                u3 = p.op("dve", lambda e, ot=ot, sl=sl: e.tensor_tensor(
                    ctx[ot][:, sl], t1s[:], t2s[:], OP.add), waits=[u1, u2])
                out_toks.append(u3)
    p.barrier()
    with ExitStack() as wos:
        wo_t = [p.sb(f"l3a_wo{k}", [128, 1024], BF16, wos) for k in range(NK)]
        lw = [p.dma("pool", wo_t[k][:], W["l3_wo"][k * 128:(k + 1) * 128, :])
              for k in range(NK)]
        fin = []
        for ot in range(NK):
            for tc in range(NTC):
                sl = slice(tc * 512, (tc + 1) * 512)
                bo = C.PS[tc % 2]
                mmo = None
                for k in range(NK):
                    mmo = p.op("pe", lambda e, k=k, ot=ot, sl=sl, bo=bo: e.matmul(
                        bo[:, 0:512], wo_t[k][:, ot * 128:(ot + 1) * 128],
                        ctx[k][:, sl], start=(k == 0), stop=False),
                        waits=lw + out_toks + [C.pstok[tc % 2], lbo])
                mmo = p.op("pe", lambda e, ot=ot, sl=sl, bo=bo: e.matmul(
                    bo[:, 0:512], bo_row[0:1, ot * 128:(ot + 1) * 128],
                    C.ones_b[0:1, 0:512], start=False, stop=True), waits=[mmo])
                evo = p.op("dve", lambda e, ot=ot, sl=sl, bo=bo: e.tensor_tensor(
                    X[ot][:, sl], bo[:, 0:512], X[ot][:, sl], OP.add), waits=[mmo])
                C.pstok[tc % 2] = evo
                fin.append(evo)
        p.barrier()
    return fin


def emit_l3_kv_tail(p, C, W, X, xtok):
    with ExitStack() as ls:
        xn, _, xn_toks, _ = emit_module_ln(p, C, X, xtok, ls, tag="Lkv")
        brows = p.sb("lkv_brows", [1, 2048], BF16, ls)
        lbr = [p.dma("pool", brows[0:1, i * 1024:(i + 1) * 1024], W[nm])
               for i, nm in enumerate(("l3_bk_r", "l3_bv_r"))]
        wld = p.sb("lkv_wld", [128, 1024], BF16, ls)
        res = p.sb("lkv_res", [128, T], BF16, ls)
        prev = None
        for wi, wnm in enumerate(("l3_wk", "l3_wv")):
            for ot in range(NK):
                lw = [p.dma("pool", wld[:, k * 128:(k + 1) * 128],
                            W[wnm][k * 128:(k + 1) * 128, ot * 128:(ot + 1) * 128],
                            waits=[prev] if prev else [])
                      for k in range(NK)]
                evs = []
                for tc in range(NTC):
                    sl = slice(tc * 512, (tc + 1) * 512)
                    bq = C.PS[4 + (tc % 2)]
                    mmq = None
                    for k in range(NK):
                        mmq = p.op("pe", lambda e, k=k, sl=sl, bq=bq: e.matmul(
                            bq[:, 0:512], wld[:, k * 128:(k + 1) * 128],
                            xn[k][:, sl], start=(k == 0), stop=False),
                            waits=lw + xn_toks + [C.pstok[4 + (tc % 2)]] + lbr)
                    mmq = p.op("pe", lambda e, wi=wi, sl=sl, bq=bq, ot=ot: e.matmul(
                        bq[:, 0:512],
                        brows[0:1, wi * 1024 + ot * 128:wi * 1024 + (ot + 1) * 128],
                        C.ones_b[0:1, 0:512], start=False, stop=True), waits=[mmq])
                    ev = p.op("dve", lambda e, sl=sl, bq=bq: e.tensor_copy(
                        res[:, sl], bq[:, 0:512]), waits=[mmq])
                    C.pstok[4 + (tc % 2)] = ev
                    evs.append(ev)
                prev = p.dma("sp", W["l3kv"][wi * 1024 + ot * 128:
                                             wi * 1024 + (ot + 1) * 128, :],
                             res[:], waits=evs, sem="dmo")
        p.barrier()


def emit_l2_attn(p, C, W, X, xn, ratio_row, xn_toks, ls):
    out_toks = []
    H, CH = 8, 16
    gsc = ExitStack()
    w1c, w1t = load_cols(p, C, W["l2_ln1w_c"], D, gsc, "l2_w1c")
    b1c32, b1t0 = load_cols(p, C, W["l2_ln1b_c"], D, gsc, "l2_b1c")
    b1cbt = p.sb("l2_b1cb", [128, NK], BF16, gsc)
    cb1 = p.op("dve", lambda e: e.tensor_copy(b1cbt[:], b1c32[0].tensor.ap()),
               waits=b1t0)
    b1c = [b1cbt[:, k:k + 1] for k in range(NK)]
    b1t = [cb1]
    ratio_rep = p.sb("l2_ratrep", [128, T], F32, gsc)
    rr = rep_row(p, C, ratio_row[0:1, :], ratio_rep, xn_toks, tag="l2_rr")
    LS = p.sb("l2_ls", [128, 128], F32, gsc)
    US = p.sb("l2_us", [128, 128], F32, gsc)
    UI = p.sb("l2_ui", [128, 128], F32, gsc)
    iza = p.sb("l2_iza", [128, 128], I32, gsc)
    izb = p.sb("l2_izb", [128, 128], I32, gsc)
    z1 = p.op("pool", lambda e: e.iota(iza[:], pattern=[[1, 128]], base=0,
                                       channel_multiplier=0))
    z2 = p.op("pool", lambda e: e.iota(izb[:], pattern=[[0, 128]], base=0,
                                       channel_multiplier=1))
    z3 = p.op("dve", lambda e: e.tensor_tensor(LS[:], izb[:], iza[:], OP.is_gt),
              waits=[z1, z2])
    z4 = p.op("dve", lambda e: e.tensor_tensor(US[:], izb[:], iza[:], OP.is_lt),
              waits=[z1, z2])
    z5 = p.op("dve", lambda e: e.tensor_tensor(UI[:], izb[:], iza[:], OP.is_le),
              waits=[z1, z2])
    LSneg = p.sb("l2_lsn", [128, 128], F32, gsc)
    UIneg = p.sb("l2_uin", [128, 128], F32, gsc)
    z6 = p.op("dve", lambda e: e.tensor_tensor(LSneg[:], izb[:], iza[:], OP.is_le),
              waits=[z1, z2])
    z7 = p.op("act", lambda e: e.activation(LSneg[:], LSneg[:], AF.Copy,
                                            scale=-1e30), waits=[z6])
    z8 = p.op("dve", lambda e: e.tensor_tensor(UIneg[:], izb[:], iza[:], OP.is_gt),
              waits=[z1, z2])
    z9 = p.op("act", lambda e: e.activation(UIneg[:], UIneg[:], AF.Copy,
                                            scale=-1e30), waits=[z8])
    mtok = [z3, z4, z5, z7, z9]

    bsig = p.sb("l2_bsig", [8, T], F32, gsc)
    laC = p.sb("l2_laC", [8, T], F32, gsc)
    with ExitStack() as gs:
        wraw = p.sb("l2_wraw", [128, 128], BF16, gs)
        wabf = p.sb("l2_wabf", [128, 128], BF16, gs)
        ldw = p.sb("l2_ldw", [128, 16], F32, gs)
        araw = p.sb("l2_araw", [8, T], F32, gs)
        scr8 = p.sb("l2_scr8", [8, T], F32, gs)
        bias2 = p.sb("l2_bias2", [16, 1], F32, gs)
        bb = p.sb("l2_bb", [8, 1], F32, gs)
        wt = []
        prevw = None
        for k in range(NK):
            la_ = p.dma("sp", ldw[:, 0:8], W["l2_wa"][k * 128:(k + 1) * 128, :],
                        sem="dmb", waits=[prevw] if prevw else [])
            lb_ = p.dma("sp", ldw[:, 8:16], W["l2_wb"][k * 128:(k + 1) * 128, :],
                        sem="dmb")
            c1_ = p.op("dve", lambda e, k=k: e.tensor_copy(
                wraw[:, k * 16:(k + 1) * 16], ldw[:]), waits=[la_, lb_])
            c2_ = p.op("act", lambda e, k=k: e.activation(
                wabf[:, k * 16:(k + 1) * 16], ldw[:], AF.Copy, scale=w1c[k]),
                waits=[c1_] + w1t)
            prevw = c2_
            wt.append(c2_)
        mmb = None
        for k in range(NK):
            mmb = p.op("pe", lambda e, k=k: e.matmul(
                C.PS[5][0:16, 0:1], wraw[:, k * 16:(k + 1) * 16], b1c[k],
                start=(k == 0), stop=(k == NK - 1)), waits=wt + b1t + [C.pstok[5]])
        eb = p.op("dve", lambda e: e.tensor_copy(bias2[:], C.PS[5][0:16, 0:1]),
                  waits=[mmb])
        C.pstok[5] = eb
        lbb = p.dma("sp", bb[:], bias2[8:16, 0:1], sem="dmb", waits=[eb])
        evl = None
        for which, dst in ((0, araw), (1, scr8)):
            for tc in range(NTC):
                sl = slice(tc * 512, (tc + 1) * 512)
                bq = C.PS[0]
                mmq = None
                for k in range(NK):
                    mmq = p.op("pe", lambda e, k=k, sl=sl, which=which, bq=bq:
                               e.matmul(bq[0:8, 0:512],
                                        wabf[:, k * 16 + which * 8:
                                             k * 16 + which * 8 + 8],
                                        xn[k][:, sl], start=(k == 0),
                                        stop=(k == NK - 1)),
                               waits=wt + xn_toks + [C.pstok[0]])
                evl = p.op("dve", lambda e, sl=sl, dst=dst, bq=bq: e.tensor_tensor(
                    dst[0:8, sl], bq[0:8, 0:512], ratio_rep[0:8, sl], OP.mult),
                    waits=[mmq] + rr)
                C.pstok[0] = evl
        ba1 = p.op("dve", lambda e: e.tensor_scalar(
            araw[:], araw[:], bias2[0:8, 0:1], None, OP.add), waits=[evl, eb])
        ba2 = p.op("dve", lambda e: e.tensor_scalar(
            scr8[:], scr8[:], bb[0:8, 0:1], None, OP.add), waits=[ba1, lbb])
        sg = p.op("act", lambda e: e.activation(bsig[:], scr8[:], AF.Sigmoid),
                  waits=[ba2])
        l1_ = p.op("act", lambda e: e.activation(scr8[:], araw[:], AF.Relu,
                                                 scale=-1.0), waits=[ba2, sg])
        l2_ = p.op("act", lambda e: e.activation(araw[:], araw[:], AF.Abs),
                   waits=[l1_])
        l3_ = p.op("act", lambda e: e.activation(araw[:], araw[:], AF.Exp,
                                                 scale=-1.0), waits=[l2_])
        l4_ = p.op("act", lambda e: e.activation(araw[:], araw[:], AF.Ln,
                                                 bias=C.cst[0:8, 4:5]), waits=[l3_])
        l5_ = p.op("dve", lambda e: e.tensor_tensor(araw[:], araw[:], scr8[:],
                                                    OP.add), waits=[l4_])
        l6_ = p.op("dve", lambda e: e.tensor_scalar(araw[:], araw[:], -1.0, None,
                                                    OP.mult), waits=[l5_])
        sct = l6_
        for c in range(CH):
            sct = p.op("dve", lambda e, c=c: e.tensor_tensor_scan(
                laC[0:8, c * 128:(c + 1) * 128], C.ones_f[0:8, 0:128],
                araw[0:8, c * 128:(c + 1) * 128], 0.0, OP.mult, OP.add),
                waits=[sct] + C.ready)
        p.barrier()

    OT = [p.sb(f"l2_ot{h}", [128, T], BF16, gsc) for h in range(H)]
    ot_done = []
    for h in range(H):
        with ExitStack() as hx:
            S0 = p.sb("l2_s0", [128, 128], F32, hx)
            S0b = p.sb("l2_s0b", [128, 128], BF16, hx)
            sz0a = p.op("dve", lambda e: e.memset(S0[:], 0.0))
            sz0 = p.op("dve", lambda e: e.memset(S0b[:], 0.0), waits=[sz0a])
            qf = p.sb("l2_qf", [128, T], BF16, hx)
            kf = p.sb("l2_kf", [128, T], BF16, hx)
            vf = p.sb("l2_vf", [128, T], BF16, hx)
            ws_ = ExitStack()
            wqh = p.sb("l2_wqh", [128, 1024], BF16, ws_)
            wkh = p.sb("l2_wkh", [128, 1024], BF16, ws_)
            wvh = p.sb("l2_wvh", [128, 1024], BF16, ws_)
            wldh = p.sb("l2_wldh", [128, 128], F32, ws_)
            wrawh = p.sb("l2_wrawh", [128, 128], BF16, ws_)
            biash = p.sb("l2_biash", [128, 3], F32, ws_)
            sqs = p.sb("l2_sqsb", [128, 512], BF16, ws_)
            prescr = p.sb("l2_pres", [128, 512], F32, ws_)
            wtoks = []
            prevld = [ot_done[-1]] if ot_done else []
            for wi, (wnm, dstw) in enumerate((("l2_wq", wqh), ("l2_wk", wkh),
                                              ("l2_wv", wvh))):
                mmb2 = None
                for k in range(NK):
                    ld = p.dma("sp", wldh[:], W[wnm][k * 128:(k + 1) * 128,
                                                     h * 128:(h + 1) * 128],
                               sem="dmb", waits=prevld[-1:])
                    cvr = p.op("dve", lambda e: e.tensor_copy(wrawh[:], wldh[:]),
                               waits=[ld])
                    mmb2 = p.op("pe", lambda e, k=k, wi=wi: e.matmul(
                        C.PS[5][0:128, wi:wi + 1], wrawh[:], b1c[k],
                        start=(k == 0), stop=(k == NK - 1)),
                        waits=[cvr, C.pstok[5]] + b1t)
                    cvf = p.op("act", lambda e, k=k, dstw=dstw: e.activation(
                        dstw[:, k * 128:(k + 1) * 128], wldh[:], AF.Copy,
                        scale=w1c[k]), waits=[cvr] + w1t)
                    prevld = [cvf]
                    wtoks.append(cvf)
                wtoks.append(mmb2)
            ebh = p.op("dve", lambda e: e.tensor_copy(biash[:], C.PS[5][:, 0:3]),
                       waits=wtoks)
            C.pstok[5] = ebh
            prj = []
            for wi, (src_, dst) in enumerate(((wqh, qf), (wkh, kf), (wvh, vf))):
                for tc in range(NTC):
                    sl = slice(tc * 512, (tc + 1) * 512)
                    bq = C.PS[4]
                    mmq = None
                    for k in range(NK):
                        mmq = p.op("pe", lambda e, k=k, sl=sl, src_=src_, bq=bq:
                                   e.matmul(bq[:, 0:512],
                                            src_[:, k * 128:(k + 1) * 128],
                                            xn[k][:, sl], start=(k == 0),
                                            stop=(k == NK - 1)),
                                   waits=wtoks + xn_toks + [C.pstok[4]])
                    t1_ = p.op("dve", lambda e, sl=sl, bq=bq: e.tensor_tensor(
                        prescr[:], bq[:, 0:512], ratio_rep[:, sl], OP.mult),
                        waits=[mmq] + rr)
                    C.pstok[4] = t1_
                    t2_ = p.op("act", lambda e, wi=wi, sl=sl, dst=dst: e.activation(
                        dst[:, sl], prescr[:], AF.Silu, bias=biash[:, wi:wi + 1]),
                        waits=[t1_, ebh])
                    prj.append(t2_)
            nrm = []
            for dst in (qf, kf):
                for tc in range(NTC):
                    sl = slice(tc * 512, (tc + 1) * 512)
                    q1_ = p.op("act", lambda e, sl=sl, dst=dst: e.activation(
                        sqs[:], dst[:, sl], AF.Square), waits=prj + nrm)
                    mm1 = p.op("pe", lambda e: e.matmul(
                        C.PS[3][0:1, 0:512], C.ones_b[:, 0:1], sqs[:],
                        start=True, stop=True), waits=[q1_, C.pstok[3]])
                    r1_ = p.op("act", lambda e: e.activation(
                        C.rowscr[0:1, 0:512], C.PS[3][0:1, 0:512], AF.Ln,
                        bias=C.cst[0:1, 1:2]), waits=[mm1])
                    r2_ = p.op("act", lambda e: e.activation(
                        C.rowscr[0:1, 0:512], C.rowscr[0:1, 0:512], AF.Exp,
                        scale=-0.5), waits=[r1_])
                    mm2 = p.op("pe", lambda e: e.matmul(
                        C.PS[3][:, 0:512], C.ones_f[0:1, :],
                        C.rowscr[0:1, 0:512], start=True, stop=True),
                        waits=[r2_, r1_])
                    C.pstok[3] = mm2
                    n1_ = p.op("dve", lambda e, sl=sl, dst=dst: e.tensor_tensor(
                        dst[:, sl], dst[:, sl], C.PS[3][:, 0:512], OP.mult),
                        waits=[mm2])
                    C.pstok[3] = n1_
                    nrm.append(n1_)
            ws_.close()
            stg = p.sb("l2_stg", [2, 128], F32, hx)
            colx = p.sb("l2_colx", [128, 8], F32, hx)
            rowx = p.sb("l2_rowx", [1, 384], F32, hx)
            rmin = p.sb("l2_rmin", [1, 2], F32, hx)
            KKs = p.sb("l2_kks", [128, 128], F32, hx)
            QKTs = p.sb("l2_qkt", [128, 128], F32, hx)
            Fsc = p.sb("l2_fsc", [128, 128], F32, hx)
            Nj = p.sb("l2_nj", [128, 128], BF16, hx)
            NTj = p.sb("l2_ntj", [128, 128], BF16, hx)
            Nj2 = p.sb("l2_nj2", [128, 128], BF16, hx)
            NTj2 = p.sb("l2_ntj2", [128, 128], BF16, hx)
            Vb = p.sb("l2_vb", [128, 128], BF16, hx)
            Kt = p.sb("l2_kt", [128, 128], BF16, hx)
            Ktsc = p.sb("l2_ktsc", [128, 128], BF16, hx)
            Rsb = p.sb("l2_rsb", [128, 128], F32, hx)
            Vj = p.sb("l2_vj", [128, 128], BF16, hx)
            Vj2 = p.sb("l2_vj2", [128, 128], BF16, hx)
            Qg = p.sb("l2_qg", [128, 128], BF16, hx)
            W2T = p.sb("l2_w2t", [128, 128], BF16, hx)
            Ob = p.sb("l2_ob", [128, 128], BF16, hx)
            S0sc = p.sb("l2_s0sc", [128, 128], F32, hx)
            Dsc = p.sb("l2_dsc", [128, 128], F32, hx)
            tk = [sz0] + nrm + mtok + C.ready
            for c in range(CH):
                csl = slice(c * 128, (c + 1) * 128)
                t = emit_gdn_chunk(p, C, h, csl, laC, bsig, qf, kf, vf, S0, S0b,
                                   OT[h], stg, colx, rowx, rmin, KKs, QKTs, Fsc,
                                   Nj, NTj, Nj2, NTj2, Vb, Kt, Ktsc, Rsb, Vj,
                                   Vj2, Qg, W2T, Ob, S0sc, Dsc,
                                   LS, US, UI, LSneg, UIneg, tk)
                tk = [t]
            ot_done.append(tk[0])
    p.barrier()

    X2D = W["dscr_h"]
    st2 = p.sb("l2_st2", [128, 512], BF16, gsc)
    prev_cb = None
    with ExitStack() as wos:
        wo_t = [p.sb(f"l2_wo{k}", [128, 1024], BF16, wos) for k in range(NK)]
        lw = [p.dma("pool", wo_t[k][:], W["l2_wo"][k * 128:(k + 1) * 128, :])
              for k in range(NK)]
        for ot in range(NK):
            for tc in range(NTC):
                sl = slice(tc * 512, (tc + 1) * 512)
                bo = C.PS[tc % 2]
                mmo = None
                for k in range(NK):
                    mmo = p.op("pe", lambda e, k=k, ot=ot, sl=sl, bo=bo: e.matmul(
                        bo[:, 0:512], wo_t[k][:, ot * 128:(ot + 1) * 128],
                        OT[k][:, sl], start=(k == 0), stop=(k == NK - 1)),
                        waits=lw + ot_done + [C.pstok[tc % 2]])
                t1_ = p.op("dve", lambda e, ot=ot, sl=sl, bo=bo: e.tensor_tensor(
                    st2[:], bo[:, 0:512], xn[ot][:, sl], OP.add),
                    waits=[mmo] + xn_toks + ([prev_cb] if prev_cb else []))
                C.pstok[tc % 2] = t1_
                prev_cb = p.dma("sp", X2D[ot * 128:(ot + 1) * 128, sl], st2[:],
                                waits=[t1_], sem="dmb")
        p.barrier()
    gsc.close()

    w2c, w2t = load_cols(p, C, W["l2_ln2w_c"], D, ls, "l2_w2c")
    b2c32, b2t0 = load_cols(p, C, W["l2_ln2b_c"], D, ls, "l2_b2c")
    b2cbt = p.sb("l2_b2cb", [128, NK], BF16, ls)
    cb2 = p.op("dve", lambda e: e.tensor_copy(b2cbt[:], b2c32[0].tensor.ap()),
               waits=b2t0)
    b2cc = [b2cbt[:, k:k + 1] for k in range(NK)]
    b2t = [cb2]
    sqb2 = p.sb("l2_sqb2", [128, 512], BF16, ls)
    ldx2 = p.sb("l2_ldx2", [128, 512], BF16, ls)
    RS = lambda i: C.rowscr[0:1, i * 512:(i + 1) * 512]
    mu2 = p.sb("l2_mu2", [1, T], F32, ls)
    rstd2 = p.sb("l2_rstd2", [1, T], F32, ls)
    r2toks = []
    for tc in range(NTC):
        sl = slice(tc * 512, (tc + 1) * 512)
        mm1 = mm2 = None
        sq = None
        for k in range(NK):
            lx = p.dma("sp", ldx2[:], X2D[k * 128:(k + 1) * 128, sl], sem="dmb",
                       waits=[sq] if sq else [prev_cb])
            mm1 = p.op("pe", lambda e, k=k: e.matmul(
                C.PS[2][0:1, 0:512], C.ones_b[:, 0:1], ldx2[:],
                start=(k == 0), stop=(k == NK - 1)), waits=[lx, C.pstok[2]])
            sq = p.op("act", lambda e, k=k: e.activation(
                sqb2[:], ldx2[:], AF.Square), waits=[lx] + ([mm2] if mm2 else [])
                + [mm1])
            mm2 = p.op("pe", lambda e, k=k: e.matmul(
                C.PS[3][0:1, 0:512], C.ones_b[:, 0:1], sqb2[:],
                start=(k == 0), stop=(k == NK - 1)), waits=[sq, C.pstok[3]])
        e1 = p.op("act", lambda e, sl=sl: e.activation(mu2[0:1, sl],
                                                       C.PS[2][0:1, 0:512],
                                                       AF.Copy, scale=1.0 / D),
                  waits=[mm1])
        C.pstok[2] = e1
        e2 = p.op("act", lambda e: e.activation(RS(1), C.PS[3][0:1, 0:512],
                                                AF.Copy, scale=1.0 / D),
                  waits=[mm2])
        C.pstok[3] = e2
        e3 = p.op("act", lambda e, sl=sl: e.activation(RS(4), mu2[0:1, sl],
                                                       AF.Square), waits=[e1])
        e4 = p.op("dve", lambda e: e.tensor_tensor(RS(1), RS(1), RS(4),
                                                   OP.subtract), waits=[e2, e3])
        e5 = p.op("act", lambda e: e.activation(RS(1), RS(1), AF.Ln,
                                                bias=C.cst[0:1, 0:1]), waits=[e4])
        e6 = p.op("act", lambda e, sl=sl: e.activation(rstd2[0:1, sl], RS(1),
                                                       AF.Exp, scale=-0.5),
                  waits=[e5])
        r2toks.append(e6)
    rep_mu = p.sb("l2_repmu", [128, T], F32, ls)
    rep_rs = p.sb("l2_reprs", [128, T], F32, ls)
    rm1 = rep_row(p, C, mu2[0:1, :], rep_mu, r2toks, tag="l2_rm")
    rm2 = rep_row(p, C, rstd2[0:1, :], rep_rs, r2toks, tag="l2_rs")
    x2buf = p.sb("l2_x2b", [128, T], BF16, ls)
    x2n_toks = []
    xkp = None
    for k in range(NK):
        lb2 = p.dma("sp", x2buf[:], X2D[k * 128:(k + 1) * 128, :], sem="dmb",
                    waits=[xkp] if xkp else r2toks)
        a2 = p.op("dve", lambda e, k=k: e.tensor_tensor(
            X[k][:], X[k][:], x2buf[:], OP.add), waits=[lb2])
        a3 = p.op("dve", lambda e, k=k: e.tensor_tensor(
            xn[k][:], x2buf[:], rep_mu[:], OP.subtract), waits=[a2] + rm1)
        xkp = p.op("dve", lambda e, k=k: e.tensor_tensor(
            xn[k][:], xn[k][:], rep_rs[:], OP.mult), waits=[a3] + rm2)
        x2n_toks.append(xkp)
    x2n = xn
    p.barrier()

    b2row = p.sb("l2_b2row", [1, 1024], BF16, ls)
    lb2r = p.dma("pool", b2row[:], W["l2_b2_r"])
    first8 = [True]
    for hq in range(8):
        with ExitStack() as hs8:
            w1w = [p.sb(f"l2f1_{k}", [128, 512], BF16, hs8) for k in range(NK)]
            w1r = p.sb("l2f1r", [128, 512], BF16, hs8)
            w2w = [p.sb(f"l2f2_{i}", [128, 1024], BF16, hs8) for i in range(4)]
            h2 = [p.sb(f"l2h2_{i}", [128, T], BF16, hs8) for i in range(4)]
            ld8 = p.sb("l2ld8", [128, 512], F32, hs8)
            bq1 = p.sb("l2bq1", [128, 4], F32, hs8)
            b1q = p.sb("l2b1q", [128, 4], F32, hs8)
            uw = []
            w1raw = [p.sb(f"l2f1r_{k}", [128, 512], BF16, hs8) for k in range(NK)]
            prevu = [out_toks[-1]] if out_toks else []
            for k in range(NK):
                lu = p.dma("sp", ld8[:], W["l2_w1"][k * 128:(k + 1) * 128,
                                                    hq * 512:(hq + 1) * 512],
                           sem="dmb", waits=prevu[-1:])
                cr = p.op("dve", lambda e, k=k: e.tensor_copy(w1raw[k][:], ld8[:]),
                          waits=[lu])
                cu = p.op("act", lambda e, k=k: e.activation(
                    w1w[k][:], ld8[:], AF.Copy, scale=w2c[k]),
                    waits=[cr] + w2t)
                prevu = [cu]
                uw.append(cu)
            mmbb = None
            for i in range(4):
                for k in range(NK):
                    mmbb = p.op("pe", lambda e, k=k, i=i: e.matmul(
                        C.PS[5][0:128, i:i + 1], w1raw[k][:, i * 128:(i + 1) * 128],
                        b2cc[k], start=(k == 0), stop=(k == NK - 1)),
                        waits=[uw[k], C.pstok[5]] + b2t)
            ebq = p.op("dve", lambda e: e.tensor_copy(bq1[:], C.PS[5][:, 0:4]),
                       waits=[mmbb] + uw)
            C.pstok[5] = ebq
            b1cols, b1ct = load_cols(p, C, W["l2_b1_c"], 4096, hs8, f"l2b1_{hq}")
            abq = p.op("dve", lambda e, hq=hq: e.tensor_tensor(
                b1q[:], bq1[:], b1cols[0].tensor.ap()[:, hq * 4:(hq + 1) * 4],
                OP.add), waits=[ebq] + b1ct)
            dwt = [p.dma("pool", w2w[i][:],
                         W["l2_w2"][hq * 512 + i * 128:hq * 512 + (i + 1) * 128, :])
                   for i in range(4)]
            hts = []
            for ht in range(4):
                for tc in range(NTC):
                    sl = slice(tc * 512, (tc + 1) * 512)
                    bq = C.PS[4]
                    mmu = None
                    for k in range(NK):
                        mmu = p.op("pe", lambda e, k=k, ht=ht, sl=sl, bq=bq:
                                   e.matmul(bq[:, 0:512],
                                            w1w[k][:, ht * 128:(ht + 1) * 128],
                                            x2n[k][:, sl], start=(k == 0),
                                            stop=(k == NK - 1)),
                                   waits=uw + x2n_toks + [C.pstok[4]])
                    sv = p.op("act", lambda e, ht=ht, sl=sl, bq=bq: e.activation(
                        h2[ht][:, sl], bq[:, 0:512], AF.Gelu,
                        bias=b1q[:, ht:ht + 1]), waits=[mmu, abq])
                    C.pstok[4] = sv
                    hts.append(sv)
            for ot in range(NK):
                for tc in range(NTC):
                    sl = slice(tc * 512, (tc + 1) * 512)
                    bq = C.PS[tc % 2]
                    mmd = None
                    for i in range(4):
                        last = (i == 3) and not first8[0]
                        mmd = p.op("pe", lambda e, i=i, ot=ot, sl=sl, bq=bq,
                                   last=last:
                                   e.matmul(bq[:, 0:512],
                                            w2w[i][:, ot * 128:(ot + 1) * 128],
                                            h2[i][:, sl], start=(i == 0),
                                            stop=last),
                                   waits=dwt + hts + [C.pstok[tc % 2]])
                    if first8[0]:
                        mmd = p.op("pe", lambda e, ot=ot, sl=sl, bq=bq: e.matmul(
                            bq[:, 0:512], b2row[0:1, ot * 128:(ot + 1) * 128],
                            C.ones_b[0:1, 0:512], start=False, stop=True),
                            waits=[mmd, lb2r])
                    ev = p.op("dve", lambda e, ot=ot, sl=sl, bq=bq: e.tensor_tensor(
                        X[ot][:, sl], bq[:, 0:512], X[ot][:, sl], OP.add),
                        waits=[mmd])
                    C.pstok[tc % 2] = ev
                    out_toks.append(ev)
            first8[0] = False
            p.barrier()
    return out_toks


def emit_gdn_chunk(p, C, h, csl, laC, bsig, qf, kf, vf, S0, S0b, OTh, stg, colx,
                   rowx, rmin, KKs, QKTs, Fsc, Nj, NTj, Nj2, NTj2, Vb, Kt, Ktsc,
                   Rsb, Vj, Vj2, Qg, W2T, Ob, S0sc, Dsc,
                   LS, US, UI, LSneg, UIneg, tk):
    d1 = p.dma("sp", stg[0:1, :], laC[h:h + 1, csl], sem="dmb", waits=tk)
    d2 = p.dma("sp", rowx[0:1, 256:384], bsig[h:h + 1, csl], sem="dmb", waits=tk)
    d2s = p.dma("sp", stg[1:2, :], bsig[h:h + 1, csl], sem="dmb", waits=tk)
    nb = p.op("dve", lambda e: e.tensor_scalar(rowx[0:1, 256:384],
                                               rowx[0:1, 256:384], -1.0, None,
                                               OP.mult), waits=[d2])
    g5 = p.op("dve", lambda e: e.tensor_reduce(rmin[0:1, 0:1], stg[0:1, :],
                                               axis=AX.X, op=OP.min), waits=[d1])
    t1 = p.op("pe", lambda e: e.transpose(C.PS[5][:, 0:2], stg[:],
                                          C.ident_f[0:2, 0:2]),
              waits=[d1, d2s, C.pstok[5]] + tk)
    c0 = p.op("dve", lambda e: e.tensor_copy(colx[:, 0:2], C.PS[5][:, 0:2]),
              waits=[t1])
    C.pstok[5] = c0
    c3 = p.op("dve", lambda e: e.tensor_scalar(colx[:, 4:5], colx[:, 1:2], -1.0,
                                               None, OP.mult), waits=[c0])
    mgc = p.op("pe", lambda e: e.matmul(C.PS[3][:, 0:1], C.ones_f[0:1, :],
                                        rmin[0:1, 0:1], start=True, stop=True),
               waits=[g5, C.pstok[3]])
    c4 = p.op("dve", lambda e: e.tensor_copy(colx[:, 5:6], C.PS[3][:, 0:1]),
              waits=[mgc])
    C.pstok[3] = c4
    c5 = p.op("dve", lambda e: e.tensor_tensor(colx[:, 6:7], colx[:, 0:1],
                                               colx[:, 5:6], OP.subtract),
              waits=[c4, c0])
    c6 = p.op("act", lambda e: e.activation(colx[:, 6:7], colx[:, 6:7], AF.Exp,
                                            scale=-1.0), waits=[c5])
    c7 = p.op("act", lambda e: e.activation(colx[:, 5:6], colx[:, 5:6], AF.Exp),
              waits=[c6])
    m1 = p.op("pe", lambda e: e.matmul(C.PS[0][:, 0:128], kf[:, csl], kf[:, csl],
                                       start=True, stop=True),
              waits=tk + [C.pstok[0]])
    e1 = p.op("dve", lambda e: e.tensor_copy(KKs[:], C.PS[0][:, 0:128]), waits=[m1])
    m2 = p.op("pe", lambda e: e.matmul(C.PS[0][:, 0:128], kf[:, csl], qf[:, csl],
                                       start=True, stop=True), waits=[e1])
    e2 = p.op("dve", lambda e: e.tensor_copy(QKTs[:], C.PS[0][:, 0:128]), waits=[m2])
    C.pstok[0] = e2
    rp1 = p.op("pe", lambda e: e.matmul(C.PS[3][:, 0:128], C.ones_f[0:1, :],
                                        stg[0:1, :], start=True, stop=True),
               waits=[d1, C.pstok[3]])
    x1 = p.op("dve", lambda e: e.tensor_scalar(Dsc[:], C.ones_f[:], colx[:, 0:1],
                                               None, OP.mult), waits=[c0])
    x2 = p.op("dve", lambda e: e.tensor_tensor(Dsc[:], Dsc[:], C.PS[3][:, 0:128],
                                               OP.subtract), waits=[x1, rp1])
    x3 = p.op("dve", lambda e: e.tensor_tensor(Fsc[:], Dsc[:], LSneg[:], OP.add),
              waits=[x2])
    x4 = p.op("act", lambda e: e.activation(Fsc[:], Fsc[:], AF.Exp), waits=[x3])
    n1 = p.op("dve", lambda e: e.tensor_tensor(Fsc[:], Fsc[:], KKs[:], OP.mult),
              waits=[x4, e1])
    n2 = p.op("dve", lambda e: e.tensor_scalar(Nj[:], Fsc[:], colx[:, 4:5], None,
                                               OP.mult), waits=[n1, c3])
    y1 = p.op("dve", lambda e: e.tensor_scalar(
        Dsc[:], C.ones_f[:], colx[:, 0:1], None, OP.mult), waits=[x4, c0])
    y2 = p.op("dve", lambda e: e.scalar_tensor_tensor(
        Dsc[:], C.PS[3][:, 0:128], 1.0, Dsc[:], OP.mult, OP.subtract),
        waits=[y1, rp1])
    y3 = p.op("dve", lambda e: e.tensor_tensor(Dsc[:], Dsc[:], UIneg[:], OP.add),
              waits=[y2])
    y4 = p.op("act", lambda e: e.activation(Dsc[:], Dsc[:], AF.Exp), waits=[y3])
    w3 = p.op("dve", lambda e: e.tensor_tensor(W2T[:], Dsc[:], QKTs[:], OP.mult),
              waits=[y4, e2])
    rp2 = p.op("pe", lambda e: e.matmul(C.PS[3][:, 0:128], C.ones_f[0:1, :],
                                        rowx[0:1, 256:384], start=True, stop=True),
               waits=[nb, x2, y2, C.pstok[3]])
    u1 = p.op("dve", lambda e: e.tensor_tensor(Dsc[:], Dsc[:], US[:], OP.mult),
              waits=[w3])
    u2 = p.op("dve", lambda e: e.tensor_tensor(Dsc[:], Dsc[:], KKs[:], OP.mult),
              waits=[u1])
    u3 = p.op("dve", lambda e: e.tensor_tensor(NTj[:], Dsc[:], C.PS[3][:, 0:128],
                                               OP.mult), waits=[u2, rp2])
    C.pstok[3] = u3
    t2 = p.op("pe", lambda e: e.transpose(C.PSB[:, 0:128], vf[:, csl], C.ident[:]),
              waits=tk + [C.bstok])
    e3 = p.op("dve", lambda e: e.tensor_copy(Vb[:], C.PSB[:, 0:128]), waits=[t2])
    C.bstok = e3
    m3 = p.op("pe", lambda e: e.matmul(C.PS[2][:, 0:128], kf[:, csl], S0b[:],
                                       start=True, stop=True),
              waits=tk + [C.pstok[2], e2])
    gcl = p.op("act", lambda e: e.activation(colx[:, 7:8], colx[:, 0:1], AF.Exp),
               waits=[c0])
    r1 = p.op("dve", lambda e: e.tensor_scalar(Rsb[:], C.PS[2][:, 0:128],
                                               colx[:, 7:8], None, OP.mult),
              waits=[m3, gcl])
    C.pstok[2] = r1
    r2 = p.op("dve", lambda e: e.tensor_tensor(Rsb[:], Vb[:], Rsb[:], OP.subtract),
              waits=[r1, e3])
    r3 = p.op("dve", lambda e: e.tensor_scalar(Vj[:], Rsb[:], colx[:, 1:2], None,
                                               OP.mult), waits=[r2])
    tprev = r3
    for j in range(7):
        mU = p.op("pe", lambda e: e.matmul(C.PS[2][:, 0:128], NTj[:], Vj[:],
                                           start=True, stop=True),
                  waits=[tprev, u3, C.pstok[2]])
        aU = p.op("dve", lambda e: e.tensor_tensor(Vj2[:], C.PS[2][:, 0:128],
                                                   Vj[:], OP.add), waits=[mU])
        C.pstok[2] = aU
        sw = p.op("dve", lambda e: e.tensor_copy(Vj[:], Vj2[:]), waits=[aU])
        tprev = sw
        if j < 6:
            mq1 = p.op("pe", lambda e: e.matmul(C.PS[0][:, 0:128], NTj[:], Nj[:],
                                                start=True, stop=True),
                       waits=[tprev, C.pstok[0]])
            eq1 = p.op("dve", lambda e: e.tensor_copy(Nj2[:], C.PS[0][:, 0:128]),
                       waits=[mq1])
            mq2 = p.op("pe", lambda e: e.matmul(C.PS[0][:, 0:128], Nj[:], NTj[:],
                                                start=True, stop=True), waits=[eq1])
            eq2 = p.op("dve", lambda e: e.tensor_copy(NTj2[:], C.PS[0][:, 0:128]),
                       waits=[mq2])
            C.pstok[0] = eq2
            s1 = p.op("dve", lambda e: e.tensor_copy(Nj[:], Nj2[:]), waits=[eq2])
            s2 = p.op("dve", lambda e: e.tensor_copy(NTj[:], NTj2[:]), waits=[s1])
            tprev = s2
    rp3 = p.op("pe", lambda e: e.matmul(C.PS[3][:, 0:128], C.ones_f[0:1, :],
                                        stg[0:1, :], start=True, stop=True),
               waits=[d1, u3, C.pstok[3]])
    ge = p.op("act", lambda e: e.activation(Dsc[:], C.PS[3][:, 0:128], AF.Exp),
              waits=[rp3, u3])
    C.pstok[3] = ge
    q1 = p.op("dve", lambda e: e.tensor_tensor(Qg[:], qf[:, csl], Dsc[:], OP.mult),
              waits=[ge] + tk)
    mo1 = p.op("pe", lambda e: e.matmul(C.PS[2][:, 0:128], Qg[:], S0b[:],
                                        start=True, stop=False),
               waits=[q1, tprev, C.pstok[2]])
    mo2 = p.op("pe", lambda e: e.matmul(C.PS[2][:, 0:128], W2T[:], Vj[:],
                                        start=False, stop=True), waits=[mo1, w3])
    e4 = p.op("dve", lambda e: e.tensor_copy(Ob[:], C.PS[2][:, 0:128]), waits=[mo2])
    C.pstok[2] = e4
    t3 = p.op("pe", lambda e: e.transpose(C.PSB[:, 0:128], Ob[:], C.ident[:]),
              waits=[e4, C.bstok])
    e5 = p.op("dve", lambda e, csl=csl: e.tensor_copy(OTh[:, csl],
                                                      C.PSB[:, 0:128]), waits=[t3])
    C.bstok = e5
    t4 = p.op("pe", lambda e: e.transpose(C.PSB[:, 128:256], kf[:, csl],
                                          C.ident[:]), waits=tk + [e5])
    e6 = p.op("dve", lambda e: e.tensor_copy(Kt[:], C.PSB[:, 128:256]), waits=[t4])
    C.bstok = e6
    k1 = p.op("dve", lambda e: e.tensor_scalar(Ktsc[:], Kt[:], colx[:, 6:7], None,
                                               OP.mult), waits=[e6, c6])
    m4 = p.op("pe", lambda e: e.matmul(C.PS[4][:, 0:128], Ktsc[:], Vj[:],
                                       start=True, stop=True),
              waits=[k1, tprev, C.pstok[4], mo2])
    s3 = p.op("dve", lambda e: e.tensor_scalar(S0sc[:], S0[:], colx[:, 5:6], None,
                                               OP.mult), waits=[c7, mo2, m3])
    s4 = p.op("dve", lambda e: e.tensor_tensor(S0[:], C.PS[4][:, 0:128], S0sc[:],
                                               OP.add), waits=[m4, s3])
    C.pstok[4] = s4
    s5 = p.op("dve", lambda e: e.tensor_copy(S0b[:], S0[:]), waits=[s4])
    return s5
